# version 04
# baseline (speedup 1.0000x reference)
import numpy as np
import ml_dtypes
from contextlib import ExitStack
import concourse.bass as bass
import concourse.mybir as mybir
from concourse.bass_utils import run_bass_kernel_spmd

F32 = mybir.dt.float32
BF16 = mybir.dt.bfloat16
AF = mybir.ActivationFunctionType
ALU = mybir.AluOpType
AX = mybir.AxisListType
nbf = ml_dtypes.bfloat16

B, S, D = 4, 4096, 1024
H, DH = 16, 64
NTOK = 2048
NQT = NTOK // 128
NST = S // 512
IFFN, NEXP, AHID, MAXK = 2688, 8, 50, 6
NIC = IFFN // 128

RROUNDS = 7
ZC0, ZC1, ZC2, ZC3 = 3.4934115949635056, -0.2996396124498922, \
    0.01051770243145956, -0.0037008912809003325
Z64T = 2.162945490949954 - ZC0
BIG = 1.0e30
LN_EPS = 1.0e-5
GELU_C = 0.7978845608028654


def build_nc():
    nc = bass.Bass("TRN2", target_bir_lowering=False)

    d_xs = nc.dram_tensor("xseq", [S, D], F32, kind="ExternalInput").ap()
    d_xo = nc.dram_tensor("xown", [NTOK, D], F32, kind="ExternalInput").ap()
    d_wqT = nc.dram_tensor("wqT", [D, D], F32, kind="ExternalInput").ap()
    d_wkT = nc.dram_tensor("wkT", [D, D], F32, kind="ExternalInput").ap()
    d_wvT = nc.dram_tensor("wvT", [D, D], F32, kind="ExternalInput").ap()
    d_bq = nc.dram_tensor("bq", [1, D], F32, kind="ExternalInput").ap()
    d_bk = nc.dram_tensor("bk", [1, D], F32, kind="ExternalInput").ap()
    d_bv = nc.dram_tensor("bv", [1, D], F32, kind="ExternalInput").ap()
    d_woT = nc.dram_tensor("woT", [D, D], F32, kind="ExternalInput").ap()
    d_bo = nc.dram_tensor("bo", [1, D], F32, kind="ExternalInput").ap()
    d_rwT = nc.dram_tensor("rwT", [D, 8], F32, kind="ExternalInput").ap()
    d_a1T = nc.dram_tensor("a1T", [D, AHID], F32, kind="ExternalInput").ap()
    d_ab1 = nc.dram_tensor("ab1", [1, AHID], F32, kind="ExternalInput").ap()
    d_a2T = nc.dram_tensor("a2T", [AHID, MAXK], F32, kind="ExternalInput").ap()
    d_ab2 = nc.dram_tensor("ab2", [1, MAXK], F32, kind="ExternalInput").ap()
    d_gT = nc.dram_tensor("gT", [NEXP, D, IFFN], F32, kind="ExternalInput").ap()
    d_uT = nc.dram_tensor("uT", [NEXP, D, IFFN], F32, kind="ExternalInput").ap()
    d_dT = nc.dram_tensor("dT", [NEXP, IFFN, D], F32, kind="ExternalInput").ap()
    d_iota = nc.dram_tensor("iota", [128, 16], F32, kind="ExternalInput").ap()
    d_ident = nc.dram_tensor("ident", [128, 128], BF16, kind="ExternalInput").ap()
    d_x1f = nc.dram_tensor("x1f", [NTOK, D], F32).ap()
    d_x1bT = nc.dram_tensor("x1bT", [D, NTOK], BF16).ap()
    d_facc = nc.dram_tensor("facc", [NTOK, D], F32).ap()
    d_y = nc.dram_tensor("y", [NTOK, D], F32, kind="ExternalOutput").ap()

    def at(name, shape, dtype, offset):
        return nc.alloc_sbuf_tensor_at(name, shape, dtype, offset=offset)

    SM = 18432
    iota_s = at("iota_s", [128, 16], F32, SM + 0)
    ident = at("ident_s", [128, 128], BF16, SM + 64)
    ones_b = at("ones_b", [1, 512], BF16, SM + 320)
    bq_s = at("bq_s", [1, D], BF16, SM + 1344)
    bo_s = at("bo_s", [1, D], BF16, SM + 3392)
    gsc = at("gsc", [128, 8 * NQT], F32, SM + 5440)
    bk_s = at("bk_s", [1, D], BF16, SM + 5952)
    bv_s = at("bv_s", [1, D], BF16, SM + 8000)
    ab1_s = at("ab1_s", [1, 64], BF16, SM + 10048)
    ab2_s = at("ab2_s", [1, 8], BF16, SM + 10176)
    a2_s = at("a2_s", [AHID, MAXK], BF16, SM + 10208)
    sel = at("sel", [128, 48], F32, SM + 10240)
    acc_sb = at("acc_sb", [128, 16], F32, SM + 10432)
    sq_sb = at("sq_sb", [128, 2], F32, SM + 10496)
    lnst = at("lnst", [128, 8], F32, SM + 10528)
    d8_sb = at("d8_sb", [128, 16], BF16, SM + 10560)
    rt_t = at("rt_t", [128, 24], F32, SM + 10592)
    lt_f = at("lt_f", [128, MAXK], F32, SM + 10688)
    gu_s = at("gu_s", [128, 256], BF16, SM + 10720)
    sg_s = at("sg_s", [128, 256], BF16, SM + 11232)
    rw_s = at("rw_s", [128, 8 * 8], BF16, SM + 11744)
    a1_s = at("a1_s", [128, 8 * 64], BF16, SM + 11872)
    ac_f = at("ac_f", [AHID, 128], F32, SM + 12896)
    ac_g = at("ac_g", [AHID, 128], F32, SM + 13408)
    ac_b = at("ac_b", [AHID, 128], BF16, SM + 13920)
    BG = 32704
    KT = at("KT", [128, 8 * 4096], BF16, BG)
    Vv = at("Vv", [128, 32 * 1024], BF16, BG + 65536)
    wq_s = at("wq_s", [128, 8 * 1024], BF16, BG + 131072)
    AR = BG + 147456

    wk_s = at("wk_s", [128, 8 * 1024], BF16, AR)
    wv_s = at("wv_s", [128, 8 * 1024], BF16, AR + 16384)
    xbst = at("xbst", [128, 4 * 1024], BF16, AR + 32768)
    xTst = at("xTst", [128, 8 * 512], BF16, AR + 40960)

    s_sb = at("s_sb", [128, 2 * 4096], BF16, AR)
    w_sb = at("w_sb", [128, 2 * 4096], BF16, AR + 16384)
    xf_t = at("xf_t", [128, D], F32, AR + 32768)
    xb_t = at("xb_t", [128, D], BF16, AR + 36864)
    xT_q = at("xT_q", [128, 8 * 128], BF16, AR + 38912)
    Qt = at("Qt", [128, 8 * 128], BF16, AR + 40960)
    attnT = at("attnT", [128, 8 * 128], BF16, AR + 43008)
    r_sb = at("r_sb", [128, D], F32, AR)
    scr4 = at("scr4", [128, D], F32, AR + 4096)
    wo_sl = at("wo_sl", [128, D], BF16, AR + 8192)

    x1T_s = at("x1T_s", [128, 8 * NTOK], BF16, BG)
    gate_s = at("gate_s", [128, 8 * IFFN], BF16, BG + 32768)
    up_s = at("up_s", [128, 8 * IFFN], BF16, BG + 75776)
    down_s = at("down_s", [128, 2 * 1024], BF16, BG + 118784)
    fst_s = at("fst_s", [128, D], F32, BG + 122880)

    pA = nc.alloc_psum_tensor("pA", [128, 1024], F32)
    pC = nc.alloc_psum_tensor("pC", [128, 512], F32)
    pD = nc.alloc_psum_tensor("pD", [128, 512], F32)
    pE = nc.alloc_psum_tensor("pE", [128, 512], F32)
    pF = nc.alloc_psum_tensor("pF", [128, 512], F32)
    pG = nc.alloc_psum_tensor("pG", [128, 512], F32)
    pT = nc.alloc_psum_tensor("pT", [128, 256], BF16)

    with ExitStack() as ctx:
        E_ = ctx.enter_context
        sems = {nm: E_(nc.semaphore(nm)) for nm in
                ("sq", "gq", "TE", "VE", "SE", "GE")}
        cnts = {nm: 0 for nm in sems}
        block = E_(nc.Block())

        ops = []
        fence_req = {"sq": 0, "gq": 0}
        fence_app = {"sq": 0, "gq": 0}
        pending_bar = {}

        SELF = {"VE": "VE", "SE": "SE", "GE": "GE"}

        def op(eng, fn, waits=(), inc=()):
            w = [(s, v) for (s, v) in waits if v > 0]
            bar = pending_bar.pop(eng, None)
            if bar:
                have = dict()
                for s, v in w:
                    have[s] = max(have.get(s, 0), v)
                for s, v in bar:
                    if v > have.get(s, 0):
                        w.append((s, v))
            inc = list(inc)
            if eng in SELF:
                sn = SELF[eng]
                if cnts[sn] > 0:
                    w = [(s, v) for (s, v) in w if s != sn] + [(sn, cnts[sn])]
                if not any(s == sn for s, n in inc):
                    inc.append((sn, 1))
            ops.append([eng, fn, w, inc])
            for s, n in inc:
                cnts[s] += n

        def tk(s):
            return cnts[s]

        def dbound(q):
            fence_req[q] = cnts[q]
            return cnts[q]

        def dma(q, dst, src, waits=()):
            w = list(waits)
            if fence_req[q] > fence_app[q]:
                w.append((q, fence_req[q]))
                fence_app[q] = fence_req[q]
            op(q, lambda e, d=dst, s=src: e.dma_start(out=d, in_=s), w,
               [(q, 16)])

        def ensure_inc(engname):
            for rec in reversed(ops):
                if rec[0] == engname:
                    semn = "GE" if engname in ("gq", "GE") else engname
                    if engname in ("sq",):
                        semn = "sq"
                    if not rec[3]:
                        n = 16 if engname in ("sq", "gq") else 1
                        sn = engname if engname in ("sq", "gq") else semn
                        rec[3].append((sn, n))
                        cnts[sn] += n
                    return

        def barrier():
            for en in ("sq", "gq", "TE", "VE", "SE", "GE"):
                if any(r[0] == en for r in ops):
                    ensure_inc(en)
            dbound("sq"); dbound("gq")
            snap = {s: tk(s) for s in cnts}
            blist = [(s, v) for s, v in snap.items() if v > 0]
            for en in ("sq", "gq", "TE", "VE", "SE", "GE"):
                pending_bar[en] = [(s, v) for (s, v) in blist if s != en]
            return blist

        dma("sq", iota_s[:], d_iota)
        dma("sq", ident[:], d_ident)
        dma("gq", bq_s[:], d_bq)
        dma("gq", bk_s[:], d_bk)
        dma("gq", ab1_s[0:1, 0:AHID], d_ab1)
        dma("gq", ab2_s[0:1, 0:MAXK], d_ab2)
        dma("gq", bo_s[:], d_bo)
        dma("gq", bv_s[:], d_bv)
        dma("gq", a2_s[:], d_a2T)
        op("GE", lambda e: e.memset(ones_b[:], 1.0), [], [("GE", 1)])
        Vv4 = Vv.ap().rearrange("p (a b c) -> p a b c", b=16, c=64)
        for dc in range(8):
            dma("gq", wq_s[:, dc * 1024:(dc + 1) * 1024],
                d_wqT[dc * 128:(dc + 1) * 128, :])
            dma("gq", wk_s[:, dc * 1024:(dc + 1) * 1024],
                d_wkT[dc * 128:(dc + 1) * 128, :])
            dma("gq", wv_s[:, dc * 1024:(dc + 1) * 1024],
                d_wvT[dc * 128:(dc + 1) * 128, :])
            dma("gq", rw_s[:, dc * 8:(dc + 1) * 8],
                d_rwT[dc * 128:(dc + 1) * 128, :])
            dma("gq", a1_s[:, dc * 64: dc * 64 + AHID],
                d_a1T[dc * 128:(dc + 1) * 128, :])
        init_done = [("sq", dbound("sq")), ("gq", dbound("gq")), ("GE", tk("GE"))]

        prev_xuse = 0
        prev_vst = {}
        for st in range(NST):
            for ti in range(4):
                dma("gq", xbst[:, ti * 1024:(ti + 1) * 1024],
                    d_xs[st * 512 + ti * 128: st * 512 + (ti + 1) * 128, :],
                    waits=[("TE", prev_xuse)] if ti == 0 else [])
            ldx = dbound("gq")
            for ti in range(4):
                for j in range(8):
                    op("TE", lambda e, ti=ti, j=j: e.transpose(
                        pT[:, 0:128],
                        xbst[:, ti * 1024 + j * 128: ti * 1024 + (j + 1) * 128],
                        ident[:]),
                       [("gq", ldx), ("SE", tk("SE"))] + (init_done if (st == 0 and ti == 0 and j == 0) else []),
                       [("TE", 1)])
                    op("SE", lambda e, ti=ti, j=j: e.copy(
                        xTst[:, j * 512 + ti * 128: j * 512 + (ti + 1) * 128],
                        pT[:, 0:128]),
                       [("TE", tk("TE"))], [("SE", 1)])
            xT_done = tk("SE")
            evK = tk("SE")
            for j in range(8):
                for dc in range(8):
                    op("TE", lambda e, j=j, dc=dc: e.matmul(
                        pC[:], wk_s[:, dc * 1024 + j * 128: dc * 1024 + (j + 1) * 128],
                        xTst[:, dc * 512:(dc + 1) * 512],
                        start=(dc == 0), stop=False),
                       ([("SE", max(xT_done, evK))] +
                        (init_done if (st == 0 and j == 0) else []))
                       if dc == 0 else [], [])
                op("TE", lambda e, j=j: e.matmul(
                    pC[:], bk_s[0:1, j * 128:(j + 1) * 128],
                    ones_b[0:1, 0:512], start=False, stop=True),
                   [], [("TE", 1)])
                op("SE", lambda e, j=j, st=st: e.copy(
                    KT[:, j * 4096 + st * 512: j * 4096 + (st + 1) * 512],
                    pC[:]),
                   [("TE", tk("TE"))], [("SE", 1)])
                evK = tk("SE")
            for ti in range(4):
                tc = st * 4 + ti
                wv_prev = prev_vst.get("v", 0)
                for dc in range(8):
                    for hh in range(2):
                        op("TE", lambda e, ti=ti, dc=dc, hh=hh: e.matmul(
                            pA[:, hh * 512:(hh + 1) * 512],
                            xTst[:, dc * 512 + ti * 128: dc * 512 + (ti + 1) * 128],
                            wv_s[:, dc * 1024 + hh * 512: dc * 1024 + (hh + 1) * 512],
                            start=(dc == 0), stop=False),
                           [("SE", wv_prev)] if (dc == 0 and hh == 0) else [], [])
                for hh in range(2):
                    op("TE", lambda e, hh=hh: e.matmul(
                        pA[:, hh * 512:(hh + 1) * 512],
                        ones_b[0:1, 0:128], bv_s[0:1, hh * 512:(hh + 1) * 512],
                        start=False, stop=True),
                       [], [("TE", 1)] if hh == 1 else [])
                op("SE", lambda e, tc=tc: e.copy(
                    Vv4[:, tc, :, :],
                    pA.ap().rearrange("p (a b) -> p a b", a=16)),
                   [("TE", tk("TE"))], [("SE", 1)])
                prev_vst["v"] = tk("SE")
            prev_xuse = tk("TE")

        s1_done = barrier()
        op("VE", lambda e: e.memset(sel[:], 0.0), s1_done, [("VE", 1)])

        (MU, SIG, TAUA, ZA, TAUB, ZB, CC, BD, BT, BC, T1, T2, T3,
         AL, RK, TVAL, UB, SLO, S2HI, GE64, TPK, T4) = range(22)

        def sc(q, i=None):
            if i is None:
                return sel[:, 2 * q: 2 * q + 2]
            return sel[:, 2 * q + i: 2 * q + i + 1]

        V = "VE"; SEN = "SE"; T = "TE"

        def vop(fn, waits=(), inc=True):
            op(V, fn, waits, [("VE", 1)] if inc else [])

        def sop(fn, waits=(), inc=True):
            op(SEN, fn, waits, [("SE", 1)] if inc else [])

        for qt in range(NQT):
            if qt > 0:
                barrier()
            dma("sq", xf_t[:], d_xo[qt * 128:(qt + 1) * 128, :],
                waits=[("SE", tk("SE")), ("VE", tk("VE"))])
            dbound("sq")
            dma("gq", xb_t[:], d_xo[qt * 128:(qt + 1) * 128, :],
                waits=[("TE", tk("TE"))])
            ldxb = dbound("gq")
            for j in range(8):
                op(T, lambda e, j=j: e.transpose(
                    pT[:, 0:128], xb_t[:, j * 128:(j + 1) * 128], ident[:]),
                   [("gq", ldxb), ("SE", tk("SE"))], [("TE", 1)])
                sop(lambda e, j=j: e.copy(xT_q[:, j * 128:(j + 1) * 128],
                                          pT[:, 0:128]),
                    [("TE", tk("TE"))])
            xtq = tk("SE")
            evq = 0
            for j in range(8):
                for dc in range(8):
                    op(T, lambda e, j=j, dc=dc: e.matmul(
                        pC[:, 0:128],
                        wq_s[:, dc * 1024 + j * 128: dc * 1024 + (j + 1) * 128],
                        xT_q[:, dc * 128:(dc + 1) * 128],
                        start=(dc == 0), stop=False),
                       [("SE", max(xtq, evq))] if dc == 0 else [], [])
                op(T, lambda e, j=j: e.matmul(
                    pC[:, 0:128], bq_s[0:1, j * 128:(j + 1) * 128],
                    ones_b[0:1, 0:128], start=False, stop=True),
                   [], [("TE", 1)])
                sop(lambda e, j=j: e.copy(
                    Qt[:, j * 128:(j + 1) * 128], pC[:, 0:128]),
                    [("TE", tk("TE"))])
                evq = tk("SE")

            for pj in range(8):
                hA, hB = 2 * pj, 2 * pj + 1
                pair_ve0 = tk("VE")
                pair_te0 = tk("TE")
                first_evac = [True]
                banksAB = [(pC, pD), (pE, pF)]
                ev_tok = [0, 0]
                for kc in range(8):
                    bA, bB = banksAB[kc % 2]
                    w0 = [("SE", ev_tok[kc % 2])] if kc >= 2 else []
                    op(T, lambda e, pj=pj, kc=kc, bA=bA: e.matmul(
                        bA[:], Qt[0:64, pj * 128:(pj + 1) * 128],
                        KT[0:64, pj * 4096 + kc * 512: pj * 4096 + (kc + 1) * 512],
                        start=True, stop=True, tile_position=(0, 0)),
                       w0 + ([("SE", tk("SE"))] if kc < 2 else []), [])
                    op(T, lambda e, pj=pj, kc=kc, bB=bB: e.matmul(
                        bB[:], Qt[64:128, pj * 128:(pj + 1) * 128],
                        KT[64:128, pj * 4096 + kc * 512: pj * 4096 + (kc + 1) * 512],
                        start=True, stop=True, tile_position=(64, 0)),
                       [], [("TE", 1)])
                    tec = tk("TE")
                    for i, bX in ((0, bA), (1, bB)):
                        evw = [("TE", tec)]
                        if first_evac[0]:
                            evw.append(("VE", pair_ve0))
                            first_evac[0] = False
                        sop(lambda e, i=i, kc=kc, bX=bX: e.activation(
                            out=s_sb[:, i * 4096 + kc * 512: i * 4096 + (kc + 1) * 512],
                            in_=bX[:], func=AF.Copy, scale=1.0,
                            accum_out=acc_sb[:, i * 8 + kc: i * 8 + kc + 1]),
                            evw)
                        if kc == 2:
                            sop(lambda e, i=i, bX=bX: e.activation(
                                out=scr4[:, i * 512:(i + 1) * 512], in_=bX[:],
                                func=AF.Square, scale=1.0,
                                accum_out=sq_sb[:, i:i + 1]), [])
                    ev_tok[kc % 2] = tk("SE")
                ph1 = tk("SE")

                vop(lambda e: e.tensor_reduce(
                    sc(MU), acc_sb.ap().rearrange("p (a b) -> p a b", a=2),
                    axis=AX.X, op=ALU.add), [("SE", ph1), ("TE", pair_te0)])
                vop(lambda e: e.tensor_scalar(sc(MU), sc(MU), 1.0 / 4096, None,
                                              op0=ALU.mult))
                vop(lambda e: e.tensor_mul(sc(T1), sc(MU), sc(MU)))
                vop(lambda e: e.scalar_tensor_tensor(
                    sc(T1), sq_sb[:], 1.0 / 512, sc(T1),
                    op0=ALU.mult, op1=ALU.subtract))
                vop(lambda e: e.tensor_scalar(sc(T1), sc(T1), 1e-4, None,
                                              op0=ALU.max))
                sop(lambda e: e.activation(out=sc(T2), in_=sc(T1), func=AF.Ln,
                                           scale=1.0), [("VE", tk("VE"))])
                sop(lambda e: e.activation(out=sc(SIG), in_=sc(T2), func=AF.Exp,
                                           scale=0.5))
                sseed = tk("SE")
                vop(lambda e: e.scalar_tensor_tensor(
                    sc(TAUA), sc(SIG), 2.1535, sc(MU),
                    op0=ALU.mult, op1=ALU.add), [("SE", sseed)])
                vop(lambda e: e.tensor_scalar(sc(SLO), sc(SIG), 0.5, None,
                                              op0=ALU.mult))
                vop(lambda e: e.tensor_scalar(sc(S2HI), sc(SIG), 2.0, None,
                                              op0=ALU.mult))
                vop(lambda e: e.memset(sc(BD), BIG))

                for r in range(RROUNDS):
                    TAU, Z = (TAUA, ZA) if r % 2 == 0 else (TAUB, ZB)
                    TAUP, ZP = (TAUB, ZB) if r % 2 == 0 else (TAUA, ZA)
                    for i in range(2):
                        vop(lambda e, i=i, TAU=TAU: e.tensor_scalar(
                            w_sb[:, 0:4096], s_sb[:, i * 4096:(i + 1) * 4096],
                            sc(TAU, i), None, op0=ALU.is_ge, op1=ALU.add,
                            accum_out=sc(CC, i)))
                    vop(lambda e: e.tensor_scalar(sc(T1), sc(CC), 64.0, None,
                                                  op0=ALU.subtract))
                    vop(lambda e: e.tensor_mul(sc(T2), sc(T1), sc(T1)))
                    vop(lambda e: e.tensor_tensor(sc(T3), sc(T2), sc(BD),
                                                  op=ALU.is_lt))
                    vop(lambda e: e.tensor_tensor(sc(BD), sc(T2), sc(BD),
                                                  op=ALU.min))
                    vop(lambda e, TAU=TAU: e.tensor_tensor(sc(T4), sc(TAU),
                                                           sc(BT), op=ALU.subtract))
                    vop(lambda e: e.tensor_mul(sc(T4), sc(T4), sc(T3)))
                    vop(lambda e: e.tensor_add(sc(BT), sc(BT), sc(T4)))
                    vop(lambda e: e.tensor_tensor(sc(T4), sc(CC), sc(BC),
                                                  op=ALU.subtract))
                    vop(lambda e: e.tensor_mul(sc(T4), sc(T4), sc(T3)))
                    vop(lambda e: e.tensor_add(sc(BC), sc(BC), sc(T4)))
                    if r == RROUNDS - 1:
                        break
                    vop(lambda e: e.tensor_scalar(sc(T1), sc(CC), 0.7, 1300.0,
                                                  op0=ALU.max, op1=ALU.min))
                    sop(lambda e: e.activation(out=sc(T2), in_=sc(T1),
                                               func=AF.Ln, scale=1.0),
                        [("VE", tk("VE"))])
                    lnr = tk("SE")
                    vop(lambda e: e.tensor_scalar(sc(T1), sc(T2), ZC3, ZC2,
                                                  op0=ALU.mult, op1=ALU.add),
                        [("SE", lnr)])
                    vop(lambda e: e.tensor_mul(sc(T1), sc(T1), sc(T2)))
                    vop(lambda e, Z=Z: e.scalar_tensor_tensor(
                        sc(Z), sc(T1), ZC1, sc(T2), op0=ALU.add, op1=ALU.mult))
                    if r == 0:
                        vop(lambda e, Z=Z: e.tensor_scalar(
                            sc(T1), sc(Z), ZC0, None, op0=ALU.add))
                        vop(lambda e: e.reciprocal(sc(T2), sc(T1)))
                        vop(lambda e: e.tensor_mul(sc(T2), sc(T2), sc(SIG)))
                        vop(lambda e: e.tensor_scalar(sc(T2), sc(T2), 2.1535,
                                                      None, op0=ALU.mult))
                    else:
                        vop(lambda e, Z=Z, ZP=ZP: e.tensor_tensor(
                            sc(T1), sc(Z), sc(ZP), op=ALU.subtract))
                        vop(lambda e: e.tensor_mul(sc(T2), sc(T1), sc(T1)))
                        vop(lambda e: e.tensor_scalar(sc(T2), sc(T2), 1e-8,
                                                      None, op0=ALU.add))
                        vop(lambda e: e.reciprocal(sc(T4), sc(T2)))
                        vop(lambda e: e.tensor_mul(sc(T1), sc(T1), sc(T4)))
                        vop(lambda e, TAU=TAU, TAUP=TAUP: e.tensor_tensor(
                            sc(T2), sc(TAU), sc(TAUP), op=ALU.subtract))
                        vop(lambda e: e.tensor_mul(sc(T2), sc(T2), sc(T1)))
                    vop(lambda e: e.tensor_tensor(sc(T2), sc(T2), sc(S2HI),
                                                  op=ALU.min))
                    vop(lambda e: e.tensor_tensor(sc(T2), sc(T2), sc(SLO),
                                                  op=ALU.max))
                    vop(lambda e, Z=Z: e.tensor_scalar(
                        sc(T1), sc(Z), Z64T, None, op0=ALU.subtract))
                    vop(lambda e: e.tensor_mul(sc(T1), sc(T1), sc(T2)))
                    vop(lambda e, TAU=TAU, TAUP=TAUP: e.tensor_tensor(
                        sc(TAUP), sc(TAU), sc(T1), op=ALU.subtract))

                vop(lambda e: e.tensor_scalar(sc(GE64), sc(BC), 64.0, None,
                                              op0=ALU.is_ge))
                vop(lambda e: e.tensor_scalar(sc(AL), sc(GE64), -2.0, 1.0,
                                              op0=ALU.mult, op1=ALU.add))
                vop(lambda e: e.tensor_scalar(sc(T1), sc(BC), 64.0, None,
                                              op0=ALU.subtract))
                vop(lambda e: e.tensor_scalar(sc(T2), sc(BC), -1.0, 63.0,
                                              op0=ALU.mult, op1=ALU.add))
                vop(lambda e: e.tensor_tensor(sc(T4), sc(T1), sc(T2),
                                              op=ALU.subtract))
                vop(lambda e: e.tensor_mul(sc(T4), sc(T4), sc(GE64)))
                vop(lambda e: e.tensor_add(sc(RK), sc(T2), sc(T4)))
                vop(lambda e: e.tensor_scalar(sc(RK), sc(RK), 0.0, 7.0,
                                              op0=ALU.max, op1=ALU.min))
                vop(lambda e: e.tensor_mul(sc(UB), sc(AL), sc(BT)))
                for i in range(2):
                    vop(lambda e, i=i: e.tensor_scalar(
                        w_sb[:, 4096:8192], s_sb[:, i * 4096:(i + 1) * 4096],
                        sc(AL, i), None, op0=ALU.mult))
                    vop(lambda e, i=i: e.tensor_scalar(
                        w_sb[:, 0:4096], w_sb[:, 4096:8192], sc(UB, i), BIG,
                        op0=ALU.is_ge, op1=ALU.mult))
                    vop(lambda e: e.tensor_sub(w_sb[:, 4096:8192],
                                               w_sb[:, 4096:8192],
                                               w_sb[:, 0:4096]))
                    vop(lambda e, i=i: e.max(out=d8_sb[:, i * 8:(i + 1) * 8],
                                             in_=w_sb[:, 4096:8192]))
                    vop(lambda e, i=i: e.tensor_copy(
                        rt_t[:, 8:16], d8_sb[:, i * 8:(i + 1) * 8]))
                    vop(lambda e, i=i: e.tensor_scalar(
                        rt_t[:, 0:8], iota_s[:, 0:8], sc(RK, i), None,
                        op0=ALU.is_equal))
                    vop(lambda e: e.tensor_mul(rt_t[:, 16:24], rt_t[:, 0:8],
                                               rt_t[:, 8:16]))
                    vop(lambda e, i=i: e.reduce_sum(sc(TPK, i), rt_t[:, 16:24],
                                                    axis=AX.X))
                vop(lambda e: e.tensor_mul(sc(TVAL), sc(TPK), sc(AL)))

                for i, h in ((0, hA), (1, hB)):
                    wreg = w_sb[:, i * 4096:(i + 1) * 4096]
                    wtreg = s_sb[:, i * 4096:(i + 1) * 4096]
                    vop(lambda e, i=i: e.max(out=d8_sb[:, i * 8:(i + 1) * 8],
                                             in_=s_sb[:, i * 4096:(i + 1) * 4096]))
                    vop(lambda e, i=i: e.tensor_copy(
                        sc(T1, i), d8_sb[:, i * 8: i * 8 + 1]))
                    vop(lambda e, i=i: e.tensor_scalar(
                        sc(T2, i), sc(T1, i), -650.0, None, op0=ALU.add))
                    vop(lambda e, i=i: e.tensor_tensor(
                        sc(T2, i), sc(T2, i), sc(TVAL, i), op=ALU.max))
                    vop(lambda e, i=i: e.tensor_scalar(
                        sc(T3, i), sc(T2, i), -0.125, None, op0=ALU.mult))
                    veSH = tk("VE")
                    sop(lambda e, i=i, wreg=wreg: e.activation(
                        out=wreg, in_=s_sb[:, i * 4096:(i + 1) * 4096],
                        func=AF.Exp, scale=0.125, bias=sc(T3, i)),
                        [("VE", veSH)])
                    seE = tk("SE")
                    vop(lambda e, i=i, wreg=wreg: e.scalar_tensor_tensor(
                        wreg, s_sb[:, i * 4096:(i + 1) * 4096], sc(TVAL, i),
                        wreg, op0=ALU.is_ge, op1=ALU.mult,
                        accum_out=sc(T4, i)), [("SE", seE)])
                    vop(lambda e, i=i: e.reciprocal(sc(T1, i), sc(T4, i)))
                    vop(lambda e, i=i, wreg=wreg: e.tensor_scalar(
                        wreg, wreg, sc(T1, i), None, op0=ALU.mult))
                    wnorm = tk("VE")
                    ev_tok2 = [0, 0]
                    wv_done = 0
                    for kc2 in range(32):
                        sl = kc2 % 2
                        op(T, lambda e, kc2=kc2, sl=sl, wreg=wreg: e.transpose(
                            pT[:, sl * 128:(sl + 1) * 128],
                            wreg[:, kc2 * 128:(kc2 + 1) * 128], ident[:]),
                           [("VE", max(wnorm, ev_tok2[sl]))], [("TE", 1)])
                        trk = tk("TE")
                        vop(lambda e, kc2=kc2, sl=sl, wtreg=wtreg: e.tensor_copy(
                            wtreg[:, kc2 * 128:(kc2 + 1) * 128],
                            pT[:, sl * 128:(sl + 1) * 128]),
                            [("TE", trk)])
                        ev_tok2[sl] = tk("VE")
                        op(T, lambda e, kc2=kc2, h=h, wtreg=wtreg: e.matmul(
                            pG[0:64, 0:128], Vv4[:, kc2, h, 0:64],
                            wtreg[:, kc2 * 128:(kc2 + 1) * 128],
                            start=(kc2 == 0), stop=(kc2 == 31)),
                           [("VE", ev_tok2[sl])] +
                           ([("SE", tk("SE"))] if kc2 == 0 else []),
                           [("TE", 1)] if kc2 == 31 else [])
                    teW = tk("TE")
                    sop(lambda e, h=h: e.copy(
                        attnT[64 * (h % 2):64 * (h % 2) + 64,
                              (h // 2) * 128:(h // 2) * 128 + 128],
                        pG[0:64, 0:128]), [("TE", teW)])

            attn_done = tk("SE")
            opj_prev = 0
            for j in range(8):
                dma("gq", wo_sl[:], d_woT[j * 128:(j + 1) * 128, :],
                    waits=[("TE", opj_prev)] +
                          ([("SE", attn_done), ("VE", tk("VE"))] if j == 0 else []))
                ldw = dbound("gq")
                for hh in range(2):
                    op(T, lambda e, j=j, hh=hh: e.matmul(
                        pA[:, hh * 512:(hh + 1) * 512],
                        attnT[:, j * 128:(j + 1) * 128],
                        wo_sl[:, hh * 512:(hh + 1) * 512],
                        start=(j == 0), stop=False),
                       ([("gq", ldw), ("SE", attn_done)] +
                        ([("VE", tk("VE"))] if j == 0 else []))
                       if hh == 0 else [],
                       [("TE", 1)] if hh == 1 else [])
                opj_prev = tk("TE")
            for hh in range(2):
                op(T, lambda e, hh=hh: e.matmul(
                    pA[:, hh * 512:(hh + 1) * 512],
                    ones_b[0:1, 0:128], bo_s[0:1, hh * 512:(hh + 1) * 512],
                    start=False, stop=True), [], [("TE", 1)] if hh == 1 else [])
            teO = tk("TE")
            vop(lambda e: e.scalar_tensor_tensor(
                r_sb[:], xf_t[:], 2.0, pA.ap(),
                op0=ALU.mult, op1=ALU.add, accum_out=lnst[:, 0:1]),
                [("TE", teO), ("sq", dbound("sq"))])
            veR = tk("VE")
            sop(lambda e: e.activation(out=scr4[:], in_=r_sb[:], func=AF.Square,
                                       scale=1.0, accum_out=lnst[:, 1:2]),
                [("VE", veR)])
            seR = tk("SE")
            vop(lambda e: e.tensor_scalar(lnst[:, 2:3], lnst[:, 0:1],
                                          1.0 / 1024, None, op0=ALU.mult),
                [("SE", seR)])
            vop(lambda e: e.tensor_mul(lnst[:, 3:4], lnst[:, 2:3], lnst[:, 2:3]))
            vop(lambda e: e.scalar_tensor_tensor(
                lnst[:, 3:4], lnst[:, 1:2], 1.0 / 1024, lnst[:, 3:4],
                op0=ALU.mult, op1=ALU.subtract))
            vop(lambda e: e.tensor_scalar(lnst[:, 3:4], lnst[:, 3:4], LN_EPS,
                                          None, op0=ALU.add))
            sop(lambda e: e.activation(out=lnst[:, 4:5], in_=lnst[:, 3:4],
                                       func=AF.Ln, scale=1.0),
                [("VE", tk("VE"))])
            sop(lambda e: e.activation(out=lnst[:, 5:6], in_=lnst[:, 4:5],
                                       func=AF.Exp, scale=-0.5))
            seV = tk("SE")
            vop(lambda e: e.tensor_scalar(scr4[:], r_sb[:], lnst[:, 2:3],
                                          lnst[:, 5:6], op0=ALU.subtract,
                                          op1=ALU.mult), [("SE", seV)])
            vop(lambda e: e.tensor_scalar(xb_t[:], r_sb[:], lnst[:, 2:3],
                                          lnst[:, 5:6], op0=ALU.subtract,
                                          op1=ALU.mult))
            seX1 = tk("VE")
            dma("sq", d_x1f[qt * 128:(qt + 1) * 128, :], scr4[:],
                waits=[("VE", seX1)])
            for j in range(8):
                op(T, lambda e, j=j: e.transpose(
                    pT[:, 0:128], xb_t[:, j * 128:(j + 1) * 128], ident[:]),
                   ([("VE", seX1)] if j == 0 else []) + [("SE", tk("SE"))],
                   [("TE", 1)])
                sop(lambda e, j=j: e.copy(xT_q[:, j * 128:(j + 1) * 128],
                                          pT[:, 0:128]),
                    [("TE", tk("TE"))])
            seXT = tk("SE")
            for j in range(8):
                dma("sq", d_x1bT[j * 128:(j + 1) * 128,
                                 qt * 128:(qt + 1) * 128],
                    xT_q[:, j * 128:(j + 1) * 128],
                    waits=[("SE", seXT)] if j == 0 else [])

            for dc in range(8):
                op(T, lambda e, dc=dc: e.matmul(
                    pC[:, 0:8], xT_q[:, dc * 128:(dc + 1) * 128],
                    rw_s[:, dc * 8:(dc + 1) * 8],
                    start=(dc == 0), stop=(dc == 7)),
                   [("SE", seXT), ("VE", tk("VE"))] if dc == 0 else [],
                   [("TE", 1)] if dc == 7 else [])
            teR = tk("TE")
            vop(lambda e: e.reduce_max(rt_t[:, 8:9], pC[:, 0:8], axis=AX.X),
                [("TE", teR)])
            vop(lambda e: e.tensor_scalar(rt_t[:, 8:9], rt_t[:, 8:9], -1.0,
                                          None, op0=ALU.mult))
            veM = tk("VE")
            sop(lambda e: e.activation(out=rt_t[:, 0:8], in_=pC[:, 0:8],
                                       func=AF.Exp, bias=rt_t[:, 8:9],
                                       scale=1.0, accum_out=rt_t[:, 9:10]),
                [("VE", veM)])
            seP = tk("SE")
            vop(lambda e: e.reciprocal(rt_t[:, 10:11], rt_t[:, 9:10]),
                [("SE", seP)])
            vop(lambda e: e.tensor_scalar(rt_t[:, 0:8], rt_t[:, 0:8],
                                          rt_t[:, 10:11], None, op0=ALU.mult))
            probs_done = tk("VE")

            for dc in range(8):
                op(T, lambda e, dc=dc: e.matmul(
                    pD[0:AHID, 0:128], a1_s[:, dc * 64: dc * 64 + AHID],
                    xT_q[:, dc * 128:(dc + 1) * 128],
                    start=(dc == 0), stop=False),
                   [("TE", teR)] if dc == 0 else [], [])
            op(T, lambda e: e.matmul(
                pD[0:AHID, 0:128], ab1_s[0:1, 0:AHID], ones_b[0:1, 0:128],
                start=False, stop=True), [], [("TE", 1)])
            teA = tk("TE")
            sop(lambda e: e.copy(ac_f[:], pD[0:AHID, 0:128]), [("TE", teA)])
            seA1 = tk("SE")
            sop(lambda e: e.activation(out=ac_g[:], in_=ac_f[:],
                                       func=AF.Square, scale=1.0))
            seA2 = tk("SE")
            vop(lambda e: e.tensor_mul(ac_g[:], ac_g[:], ac_f[:]),
                [("SE", seA2)])
            vop(lambda e: e.scalar_tensor_tensor(
                ac_g[:], ac_g[:], 0.044715, ac_f[:],
                op0=ALU.mult, op1=ALU.add))
            veA = tk("VE")
            sop(lambda e: e.activation(out=ac_g[:], in_=ac_g[:], func=AF.Tanh,
                                       scale=GELU_C), [("VE", veA)])
            seA3 = tk("SE")
            vop(lambda e: e.scalar_tensor_tensor(
                ac_g[:], ac_g[:], 1.0, ac_f[:], op0=ALU.add, op1=ALU.mult),
                [("SE", seA3)])
            vop(lambda e: e.tensor_scalar(ac_b[:], ac_g[:], 0.5, None,
                                          op0=ALU.mult))
            veG = tk("VE")
            op(T, lambda e: e.matmul(pC[0:MAXK, 128:256], a2_s[:], ac_b[:],
                                     start=True, stop=False),
               [("VE", veG)], [])
            op(T, lambda e: e.matmul(
                pC[0:MAXK, 128:256], ab2_s[0:1, 0:MAXK], ones_b[0:1, 0:128],
                start=False, stop=True), [], [("TE", 1)])
            teL = tk("TE")
            sop(lambda e: e.copy(ac_f[0:MAXK, 0:128], pC[0:MAXK, 128:256]),
                [("TE", teL)])
            seL = tk("SE")
            vop(lambda e: e.tensor_scalar(ac_b[0:MAXK, 0:128],
                                          ac_f[0:MAXK, 0:128], 30.0, -30.0,
                                          op0=ALU.min, op1=ALU.max),
                [("SE", seL)])
            veL = tk("VE")
            op(T, lambda e: e.transpose(pT[0:128, 0:MAXK],
                                        ac_b[0:MAXK, 0:128], ident[0:MAXK, 0:MAXK]),
               [("VE", veL), ("SE", tk("SE"))], [("TE", 1)])
            sop(lambda e: e.copy(lt_f[:], pT[0:128, 0:MAXK]),
                [("TE", tk("TE"))])
            seLT = tk("SE")
            vop(lambda e: e.reduce_max(rt_t[:, 11:12], lt_f[:], axis=AX.X),
                [("SE", seLT)])
            vop(lambda e: e.tensor_scalar(rt_t[:, 16:22], lt_f[:],
                                          rt_t[:, 11:12], BIG,
                                          op0=ALU.is_lt, op1=ALU.mult))
            vop(lambda e: e.tensor_add(rt_t[:, 16:22], rt_t[:, 16:22],
                                       iota_s[:, 0:MAXK]))
            vop(lambda e: e.tensor_reduce(rt_t[:, 12:13], rt_t[:, 16:22],
                                          axis=AX.X, op=ALU.min))
            vop(lambda e: e.memset(rt_t[:, 16:24], 0.0))
            for ee in range(8):
                vop(lambda e, ee=ee: e.tensor_scalar(
                    acc_sb[:, 0:8], rt_t[:, 0:8], rt_t[:, ee:ee + 1], None,
                    op0=ALU.is_gt))
                vop(lambda e: e.tensor_add(rt_t[:, 16:24], rt_t[:, 16:24],
                                           acc_sb[:, 0:8]))
            vop(lambda e: e.tensor_scalar(rt_t[:, 16:24], rt_t[:, 16:24],
                                          -1.0, 7.0, op0=ALU.mult,
                                          op1=ALU.add))
            vop(lambda e: e.tensor_scalar(rt_t[:, 16:24], rt_t[:, 16:24],
                                          rt_t[:, 12:13], None, op0=ALU.is_le))
            vop(lambda e, qt=qt: e.tensor_tensor(
                gsc[:, qt * 8:(qt + 1) * 8], rt_t[:, 16:24], rt_t[:, 0:8],
                op=ALU.mult))

        s2a_done = barrier()

        for dc in range(8):
            dma("sq", x1T_s[:, dc * NTOK:(dc + 1) * NTOK],
                d_x1bT[dc * 128:(dc + 1) * 128, :], waits=s2a_done if dc == 0 else [])
        x1T_ld = dbound("sq")
        facc_tok = [0] * NQT
        gu_prev = 0
        fst_prev = 0
        exp_te = 0
        for e_i in range(NEXP):
            for dc in range(8):
                dma("gq", gate_s[:, dc * IFFN:(dc + 1) * IFFN],
                    d_gT[e_i, dc * 128:(dc + 1) * 128, :],
                    waits=([("TE", exp_te)] if dc == 0 else []))
                dma("gq", up_s[:, dc * IFFN:(dc + 1) * IFFN],
                    d_uT[e_i, dc * 128:(dc + 1) * 128, :])
            w_ld = dbound("gq")
            dn_prev = [0, 0]
            dn_evac = 0
            for tb in range(8):
                for ic in range(NIC):
                    dbuf = ic % 2
                    dma("gq", down_s[:, dbuf * 1024:(dbuf + 1) * 1024],
                        d_dT[e_i, ic * 128:(ic + 1) * 128, :],
                        waits=[("TE", dn_prev[dbuf])])
                    d_ld = dbound("gq")
                    for dc in range(8):
                        op(T, lambda e, e_i=e_i, tb=tb, ic=ic, dc=dc: e.matmul(
                            pC[:, 0:256],
                            gate_s[:, dc * IFFN + ic * 128: dc * IFFN + (ic + 1) * 128],
                            x1T_s[:, dc * NTOK + tb * 256: dc * NTOK + (tb + 1) * 256],
                            start=(dc == 0), stop=(dc == 7)),
                           ([("gq", w_ld), ("sq", x1T_ld), ("SE", tk("SE"))]
                            if dc == 0 else []), [])
                        op(T, lambda e, e_i=e_i, tb=tb, ic=ic, dc=dc: e.matmul(
                            pD[:, 0:256],
                            up_s[:, dc * IFFN + ic * 128: dc * IFFN + (ic + 1) * 128],
                            x1T_s[:, dc * NTOK + tb * 256: dc * NTOK + (tb + 1) * 256],
                            start=(dc == 0), stop=(dc == 7)),
                           [("VE", gu_prev)] if dc == 0 else [],
                           [("TE", 1)] if dc == 7 else [])
                    teGU = tk("TE")
                    sop(lambda e: e.activation(out=sg_s[:], in_=pC[:, 0:256],
                                               func=AF.Silu, scale=1.0),
                        [("TE", teGU)])
                    seS = tk("SE")
                    vop(lambda e: e.tensor_tensor(gu_s[:], sg_s[:],
                                                  pD[:, 0:256], op=ALU.mult),
                        [("SE", seS), ("TE", dn_prev[1 - dbuf])])
                    gu_prev = tk("VE")
                    for q4 in range(2):
                        for hh in range(2):
                            op(T, lambda e, ic=ic, q4=q4, hh=hh, dbuf=dbuf: e.matmul(
                                (pA[:, hh * 512:(hh + 1) * 512]) if q4 == 0 else
                                (pG[:, 0:512] if hh == 0 else pF[:, 0:512]),
                                gu_s[:, q4 * 128:(q4 + 1) * 128],
                                down_s[:, dbuf * 1024 + hh * 512: dbuf * 1024 + (hh + 1) * 512],
                                start=(ic == 0), stop=(ic == NIC - 1)),
                               ([("VE", gu_prev), ("gq", d_ld)] +
                                ([("SE", dn_evac)] if ic == 0 else []))
                               if (q4 == 0 and hh == 0) else [],
                               [("TE", 1)] if (q4 == 1 and hh == 1) else [])
                    dn_prev[dbuf] = tk("TE")
                teD = tk("TE")
                for q4 in range(2):
                    qtt = tb * 2 + q4
                    srcs = [pA.ap()] if q4 == 0 else [pG.ap(), pF.ap()]
                    if q4 == 0:
                        vop(lambda e, e_i=e_i, qtt=qtt: e.tensor_scalar(
                            fst_s[:], pA[:],
                            gsc[:, qtt * 8 + e_i: qtt * 8 + e_i + 1], None,
                            op0=ALU.mult),
                            [("TE", teD), ("gq", max(facc_tok[qtt], fst_prev))])
                    else:
                        vop(lambda e, e_i=e_i, qtt=qtt: e.tensor_scalar(
                            fst_s[:, 0:512], pG[:, 0:512],
                            gsc[:, qtt * 8 + e_i: qtt * 8 + e_i + 1], None,
                            op0=ALU.mult),
                            [("TE", teD), ("gq", max(facc_tok[qtt], fst_prev))])
                        vop(lambda e, e_i=e_i, qtt=qtt: e.tensor_scalar(
                            fst_s[:, 512:1024], pF[:, 0:512],
                            gsc[:, qtt * 8 + e_i: qtt * 8 + e_i + 1], None,
                            op0=ALU.mult), [])
                    seF = tk("VE")
                    fw = [("VE", seF)]
                    if fence_req["gq"] > fence_app["gq"]:
                        fw.append(("gq", fence_req["gq"]))
                        fence_app["gq"] = fence_req["gq"]
                    op("gq", lambda e, e_i=e_i, qtt=qtt: e.dma_start(
                        out=d_facc[qtt * 128:(qtt + 1) * 128, :], in_=fst_s[:],
                        accum_op=(ALU.bypass if e_i == 0 else ALU.add)),
                       fw, [("gq", 16)])
                    facc_tok[qtt] = dbound("gq")
                    fst_prev = facc_tok[qtt]
            exp_te = tk("TE")

        s2b_done = barrier()

        prev_ln = 0
        for qt in range(NQT):
            dma("sq", xf_t[:], d_x1f[qt * 128:(qt + 1) * 128, :],
                waits=(s2b_done if qt == 0 else []) + [("SE", prev_ln),
                                                       ("VE", tk("VE"))])
            dma("sq", r_sb[:], d_facc[qt * 128:(qt + 1) * 128, :])
            ld2 = dbound("sq")
            vop(lambda e: e.scalar_tensor_tensor(
                scr4[:], xf_t[:], 1.0, r_sb[:], op0=ALU.mult, op1=ALU.add,
                accum_out=lnst[:, 0:1]), [("sq", ld2), ("SE", prev_ln)])
            veR = tk("VE")
            sop(lambda e: e.activation(out=r_sb[:], in_=scr4[:], func=AF.Square,
                                       scale=1.0, accum_out=lnst[:, 1:2]),
                [("VE", veR)])
            seR = tk("SE")
            vop(lambda e: e.tensor_scalar(lnst[:, 2:3], lnst[:, 0:1],
                                          1.0 / 1024, None, op0=ALU.mult),
                [("SE", seR)])
            vop(lambda e: e.tensor_mul(lnst[:, 3:4], lnst[:, 2:3], lnst[:, 2:3]))
            vop(lambda e: e.scalar_tensor_tensor(
                lnst[:, 3:4], lnst[:, 1:2], 1.0 / 1024, lnst[:, 3:4],
                op0=ALU.mult, op1=ALU.subtract))
            vop(lambda e: e.tensor_scalar(lnst[:, 3:4], lnst[:, 3:4], LN_EPS,
                                          None, op0=ALU.add))
            sop(lambda e: e.activation(out=lnst[:, 4:5], in_=lnst[:, 3:4],
                                       func=AF.Ln, scale=1.0),
                [("VE", tk("VE"))])
            sop(lambda e: e.activation(out=lnst[:, 5:6], in_=lnst[:, 4:5],
                                       func=AF.Exp, scale=-0.5))
            seV = tk("SE")
            vop(lambda e: e.tensor_scalar(xf_t[:], scr4[:], lnst[:, 2:3],
                                          lnst[:, 5:6], op0=ALU.subtract,
                                          op1=ALU.mult), [("SE", seV)])
            prev_ln2 = tk("VE")
            dma("sq", d_y[qt * 128:(qt + 1) * 128, :], xf_t[:],
                waits=[("VE", prev_ln2)])
            dbound("sq")

        op("GE", lambda e: e.memset(ones_b[0:1, 0:1], 1.0),
           [("sq", tk("sq")), ("gq", tk("gq")), ("SE", tk("SE")),
            ("VE", tk("VE")), ("TE", tk("TE"))], [("GE", 1)])

        def stream(names):
            def body(eng):
                for engname, fn, waits, incs in ops:
                    if engname not in names:
                        continue
                    for s, v in waits:
                        eng.wait_ge(sems[s], v)
                    instr = fn(eng)
                    for s, n in incs:
                        instr.then_inc(sems[s], n)
            return body

        block.sync(stream(("sq",)))
        block.tensor(stream(("TE",)))
        block.vector(stream(("VE",)))
        block.scalar(stream(("SE",)))
        block.gpsimd(stream(("gq", "GE")))

    return nc


def _stage_inputs(inputs):
    f = np.float32
    qkv_w = np.asarray(inputs["L0_attn_w_qkv_w"], f).reshape(H, 3, DH, D)
    qkv_b = np.asarray(inputs["L0_attn_w_qkv_b"], f).reshape(H, 3, DH)
    wq = np.ascontiguousarray(qkv_w[:, 0].reshape(D, D).T)
    wk = np.ascontiguousarray(qkv_w[:, 1].reshape(D, D).T)
    wv = np.ascontiguousarray(qkv_w[:, 2].reshape(D, D).T)
    bq = np.ascontiguousarray(qkv_b[:, 0].reshape(1, D))
    bk = np.ascontiguousarray(qkv_b[:, 1].reshape(1, D))
    bv = np.ascontiguousarray(qkv_b[:, 2].reshape(1, D))
    woT = np.ascontiguousarray(np.asarray(inputs["L0_attn_w_out_w"], f).T)
    bo = np.ascontiguousarray(np.asarray(inputs["L0_attn_w_out_b"], f).reshape(1, D))
    rwT = np.ascontiguousarray(np.asarray(inputs["L0_ffn_router_w"], f).T)
    a1T = np.ascontiguousarray(np.asarray(inputs["L0_ffn_actor_w1"], f).T)
    ab1 = np.ascontiguousarray(np.asarray(inputs["L0_ffn_actor_b1"], f).reshape(1, AHID))
    a2T = np.ascontiguousarray(np.asarray(inputs["L0_ffn_actor_w2"], f).T)
    ab2 = np.ascontiguousarray(np.asarray(inputs["L0_ffn_actor_b2"], f).reshape(1, MAXK))
    gT = np.ascontiguousarray(np.asarray(inputs["L0_ffn_gate_w"], f).transpose(0, 2, 1))
    uT = np.ascontiguousarray(np.asarray(inputs["L0_ffn_up_w"], f).transpose(0, 2, 1))
    dT = np.ascontiguousarray(np.asarray(inputs["L0_ffn_down_w"], f).transpose(0, 2, 1))
    iota = np.broadcast_to(np.arange(16, dtype=f), (128, 16)).copy()
    ident = np.eye(128, dtype=f).astype(nbf)
    com = dict(wqT=wq, wkT=wk, wvT=wv, bq=bq, bk=bk, bv=bv, woT=woT, bo=bo,
               rwT=rwT, a1T=a1T, ab1=ab1, a2T=a2T, ab2=ab2,
               gT=gT, uT=uT, dT=dT, iota=iota, ident=ident)
    x = np.asarray(inputs["L0_attn_x"], f)
    in_maps = []
    for c in range(8):
        b, hf = c // 2, c % 2
        m = dict(com)
        m["xseq"] = np.ascontiguousarray(x[b])
        m["xown"] = np.ascontiguousarray(x[b, hf * NTOK:(hf + 1) * NTOK])
        in_maps.append(m)
    return in_maps


def kernel(**inputs):
    in_maps = _stage_inputs(inputs)
    nc = build_nc()
    res = run_bass_kernel_spmd(nc, in_maps, core_ids=list(range(8)))
    y = np.empty((B, S, D), np.float32)
    for c in range(8):
        b, hf = c // 2, c % 2
        y[b, hf * NTOK:(hf + 1) * NTOK] = res.results[c]["y"]
    return y


# version 07
# speedup vs baseline: 1.0797x; 1.0797x over previous
import numpy as np
import ml_dtypes
from contextlib import ExitStack
import concourse.bass as bass
import concourse.mybir as mybir
from concourse.bass_utils import run_bass_kernel_spmd

F32 = mybir.dt.float32
BF16 = mybir.dt.bfloat16
AF = mybir.ActivationFunctionType
ALU = mybir.AluOpType
AX = mybir.AxisListType
nbf = ml_dtypes.bfloat16

B, S, D = 4, 4096, 1024
H, DH = 16, 64
NTOK = 2048
NQT = NTOK // 128
NST = S // 512
IFFN, NEXP, AHID, MAXK = 2688, 8, 50, 6
NIC = IFFN // 128

RROUNDS = 7
ZC0, ZC1, ZC2, ZC3 = 3.4934115949635056, -0.2996396124498922, \
    0.01051770243145956, -0.0037008912809003325
Z64T = 2.162945490949954 - ZC0
BIG = 1.0e30
LN_EPS = 1.0e-5
GELU_C = 0.7978845608028654


def build_nc():
    nc = bass.Bass("TRN2", target_bir_lowering=False)

    d_xs = nc.dram_tensor("xseq", [S, D], F32, kind="ExternalInput").ap()
    d_xo = nc.dram_tensor("xown", [NTOK, D], F32, kind="ExternalInput").ap()
    d_wqT = nc.dram_tensor("wqT", [D, D], F32, kind="ExternalInput").ap()
    d_wkT = nc.dram_tensor("wkT", [D, D], F32, kind="ExternalInput").ap()
    d_wvT = nc.dram_tensor("wvT", [D, D], F32, kind="ExternalInput").ap()
    d_bq = nc.dram_tensor("bq", [1, D], F32, kind="ExternalInput").ap()
    d_bk = nc.dram_tensor("bk", [1, D], F32, kind="ExternalInput").ap()
    d_bv = nc.dram_tensor("bv", [1, D], F32, kind="ExternalInput").ap()
    d_woT = nc.dram_tensor("woT", [D, D], F32, kind="ExternalInput").ap()
    d_bo = nc.dram_tensor("bo", [1, D], F32, kind="ExternalInput").ap()
    d_rwT = nc.dram_tensor("rwT", [D, 8], F32, kind="ExternalInput").ap()
    d_a1T = nc.dram_tensor("a1T", [D, AHID], F32, kind="ExternalInput").ap()
    d_ab1 = nc.dram_tensor("ab1", [1, AHID], F32, kind="ExternalInput").ap()
    d_a2T = nc.dram_tensor("a2T", [AHID, MAXK], F32, kind="ExternalInput").ap()
    d_ab2 = nc.dram_tensor("ab2", [1, MAXK], F32, kind="ExternalInput").ap()
    d_gT = nc.dram_tensor("gT", [NEXP, D, IFFN], F32, kind="ExternalInput").ap()
    d_uT = nc.dram_tensor("uT", [NEXP, D, IFFN], F32, kind="ExternalInput").ap()
    d_dT = nc.dram_tensor("dT", [NEXP, IFFN, D], F32, kind="ExternalInput").ap()
    d_iota = nc.dram_tensor("iota", [128, 16], F32, kind="ExternalInput").ap()
    d_ident = nc.dram_tensor("ident", [128, 128], BF16, kind="ExternalInput").ap()
    d_x1f = nc.dram_tensor("x1f", [NTOK, D], F32).ap()
    d_x1bT = nc.dram_tensor("x1bT", [D, NTOK], BF16).ap()
    d_facc = nc.dram_tensor("facc", [NTOK, D], F32).ap()
    d_y = nc.dram_tensor("y", [NTOK, D], F32, kind="ExternalOutput").ap()

    def at(name, shape, dtype, offset):
        return nc.alloc_sbuf_tensor_at(name, shape, dtype, offset=offset)

    SM = 18432
    iota_s = at("iota_s", [128, 16], F32, SM + 0)
    ident = at("ident_s", [128, 128], BF16, SM + 64)
    ones_b = at("ones_b", [1, 512], BF16, SM + 320)
    bq_s = at("bq_s", [1, D], BF16, SM + 1344)
    bo_s = at("bo_s", [1, D], BF16, SM + 3392)
    gsc = at("gsc", [128, 8 * NQT], F32, SM + 5440)
    bk_s = at("bk_s", [1, D], BF16, SM + 5952)
    bv_s = at("bv_s", [1, D], BF16, SM + 8000)
    ab1_s = at("ab1_s", [1, 64], BF16, SM + 10048)
    ab2_s = at("ab2_s", [1, 8], BF16, SM + 10176)
    a2_s = at("a2_s", [AHID, MAXK], BF16, SM + 10208)
    sel = at("sel", [128, 48], F32, SM + 10240)
    acc_sb = at("acc_sb", [128, 16], F32, SM + 10432)
    sq_sb = at("sq_sb", [128, 2], F32, SM + 10496)
    lnst = at("lnst", [128, 8], F32, SM + 10528)
    d8_sb = at("d8_sb", [128, 16], BF16, SM + 10560)
    rt_t = at("rt_t", [128, 24], F32, SM + 10592)
    lt_f = at("lt_f", [128, MAXK], F32, SM + 10688)
    gu_s = at("gu_s", [128, 256], BF16, SM + 10720)
    sg_s = at("sg_s", [128, 256], BF16, SM + 11232)
    rw_s = at("rw_s", [128, 8 * 8], BF16, SM + 11744)
    a1_s = at("a1_s", [128, 8 * 64], BF16, SM + 11872)
    ac_f = at("ac_f", [AHID, 128], F32, SM + 12896)
    ac_g = at("ac_g", [AHID, 128], F32, SM + 13408)
    ac_b = at("ac_b", [AHID, 128], BF16, SM + 13920)
    BG = 32704
    KT = at("KT", [128, 8 * 4096], BF16, BG)
    Vv = at("Vv", [128, 32 * 1024], BF16, BG + 65536)
    wq_s = at("wq_s", [128, 8 * 1024], BF16, BG + 131072)
    AR = BG + 147456

    wk_s = at("wk_s", [128, 8 * 1024], BF16, AR)
    wv_s = at("wv_s", [128, 8 * 1024], BF16, AR + 16384)
    xbst = at("xbst", [128, 4 * 1024], BF16, AR + 32768)
    xTst = at("xTst", [128, 8 * 512], BF16, AR + 40960)

    s_sb = at("s_sb", [128, 2 * 4096], BF16, AR)
    w_sb = at("w_sb", [128, 2 * 4096], BF16, AR + 16384)
    xf_t = at("xf_t", [128, D], F32, AR + 32768)
    xb_t = at("xb_t", [128, D], BF16, AR + 36864)
    xT_q = at("xT_q", [128, 8 * 128], BF16, AR + 38912)
    Qt = at("Qt", [128, 8 * 128], BF16, AR + 40960)
    attnT = at("attnT", [128, 8 * 128], BF16, AR + 43008)
    r_sb = at("r_sb", [128, D], F32, AR)
    scr4 = at("scr4", [128, D], F32, AR + 4096)
    wo_sl = at("wo_sl", [128, D], BF16, AR + 8192)

    x1T_s = at("x1T_s", [128, 8 * NTOK], BF16, BG)
    gate_s = at("gate_s", [128, 8 * IFFN], BF16, BG + 32768)
    up_s = at("up_s", [128, 8 * IFFN], BF16, BG + 75776)
    down_s = at("down_s", [128, 2 * 1024], BF16, BG + 118784)
    fst_s = at("fst_s", [128, D], F32, BG + 122880)

    pA = nc.alloc_psum_tensor("pA", [128, 1024], F32)
    pC = nc.alloc_psum_tensor("pC", [128, 512], F32)
    pD = nc.alloc_psum_tensor("pD", [128, 512], F32)
    pE = nc.alloc_psum_tensor("pE", [128, 512], F32)
    pF = nc.alloc_psum_tensor("pF", [128, 512], F32)
    pG = nc.alloc_psum_tensor("pG", [128, 512], F32)
    pT = nc.alloc_psum_tensor("pT", [128, 256], BF16)

    with ExitStack() as ctx:
        E_ = ctx.enter_context
        sems = {nm: E_(nc.semaphore(nm)) for nm in
                ("sq", "gq", "TE", "VE", "SE", "GE")}
        cnts = {nm: 0 for nm in sems}
        block = E_(nc.Block())

        ops = []
        fence_req = {"sq": 0, "gq": 0}
        fence_app = {"sq": 0, "gq": 0}
        pending_bar = {}

        SELF = {"VE": "VE", "SE": "SE", "GE": "GE"}

        def op(eng, fn, waits=(), inc=()):
            w = [(s, v) for (s, v) in waits if v > 0]
            bar = pending_bar.pop(eng, None)
            if bar:
                have = dict()
                for s, v in w:
                    have[s] = max(have.get(s, 0), v)
                for s, v in bar:
                    if v > have.get(s, 0):
                        w.append((s, v))
            inc = list(inc)
            if eng in SELF:
                sn = SELF[eng]
                if cnts[sn] > 0:
                    w = [(s, v) for (s, v) in w if s != sn] + [(sn, cnts[sn])]
                if not any(s == sn for s, n in inc):
                    inc.append((sn, 1))
            ops.append([eng, fn, w, inc])
            for s, n in inc:
                cnts[s] += n

        def tk(s):
            return cnts[s]

        def dbound(q):
            fence_req[q] = cnts[q]
            return cnts[q]

        def dma(q, dst, src, waits=()):
            w = list(waits)
            if fence_req[q] > fence_app[q]:
                w.append((q, fence_req[q]))
                fence_app[q] = fence_req[q]
            op(q, lambda e, d=dst, s=src: e.dma_start(out=d, in_=s), w,
               [(q, 16)])

        def ensure_inc(engname):
            for rec in reversed(ops):
                if rec[0] == engname:
                    semn = "GE" if engname in ("gq", "GE") else engname
                    if engname in ("sq",):
                        semn = "sq"
                    if not rec[3]:
                        n = 16 if engname in ("sq", "gq") else 1
                        sn = engname if engname in ("sq", "gq") else semn
                        rec[3].append((sn, n))
                        cnts[sn] += n
                    return

        def barrier():
            for en in ("sq", "gq", "TE", "VE", "SE", "GE"):
                if any(r[0] == en for r in ops):
                    ensure_inc(en)
            dbound("sq"); dbound("gq")
            snap = {s: tk(s) for s in cnts}
            blist = [(s, v) for s, v in snap.items() if v > 0]
            for en in ("sq", "gq", "TE", "VE", "SE", "GE"):
                pending_bar[en] = [(s, v) for (s, v) in blist if s != en]
            return blist

        dma("sq", iota_s[:], d_iota)
        dma("sq", ident[:], d_ident)
        dma("gq", bq_s[:], d_bq)
        dma("gq", bk_s[:], d_bk)
        dma("gq", ab1_s[0:1, 0:AHID], d_ab1)
        dma("gq", ab2_s[0:1, 0:MAXK], d_ab2)
        dma("gq", bo_s[:], d_bo)
        dma("gq", bv_s[:], d_bv)
        dma("gq", a2_s[:], d_a2T)
        op("GE", lambda e: e.memset(ones_b[:], 1.0), [], [("GE", 1)])
        Vv4 = Vv.ap().rearrange("p (a b c) -> p a b c", b=16, c=64)
        for dc in range(8):
            dma("gq", wq_s[:, dc * 1024:(dc + 1) * 1024],
                d_wqT[dc * 128:(dc + 1) * 128, :])
            dma("gq", wk_s[:, dc * 1024:(dc + 1) * 1024],
                d_wkT[dc * 128:(dc + 1) * 128, :])
            dma("gq", wv_s[:, dc * 1024:(dc + 1) * 1024],
                d_wvT[dc * 128:(dc + 1) * 128, :])
            dma("gq", rw_s[:, dc * 8:(dc + 1) * 8],
                d_rwT[dc * 128:(dc + 1) * 128, :])
            dma("gq", a1_s[:, dc * 64: dc * 64 + AHID],
                d_a1T[dc * 128:(dc + 1) * 128, :])
        init_done = [("sq", dbound("sq")), ("gq", dbound("gq")), ("GE", tk("GE"))]

        prev_xuse = 0
        prev_vst = {}
        for st in range(NST):
            for ti in range(4):
                dma("gq", xbst[:, ti * 1024:(ti + 1) * 1024],
                    d_xs[st * 512 + ti * 128: st * 512 + (ti + 1) * 128, :],
                    waits=[("TE", prev_xuse)] if ti == 0 else [])
            ldx = dbound("gq")
            for ti in range(4):
                for j in range(8):
                    op("TE", lambda e, ti=ti, j=j: e.transpose(
                        pT[:, 0:128],
                        xbst[:, ti * 1024 + j * 128: ti * 1024 + (j + 1) * 128],
                        ident[:]),
                       [("gq", ldx), ("SE", tk("SE"))] + (init_done if (st == 0 and ti == 0 and j == 0) else []),
                       [("TE", 1)])
                    op("SE", lambda e, ti=ti, j=j: e.copy(
                        xTst[:, j * 512 + ti * 128: j * 512 + (ti + 1) * 128],
                        pT[:, 0:128]),
                       [("TE", tk("TE"))], [("SE", 1)])
            xT_done = tk("SE")
            evK = tk("SE")
            for j in range(8):
                for dc in range(8):
                    op("TE", lambda e, j=j, dc=dc: e.matmul(
                        pC[:], wk_s[:, dc * 1024 + j * 128: dc * 1024 + (j + 1) * 128],
                        xTst[:, dc * 512:(dc + 1) * 512],
                        start=(dc == 0), stop=False),
                       ([("SE", max(xT_done, evK))] +
                        (init_done if (st == 0 and j == 0) else []))
                       if dc == 0 else [], [])
                op("TE", lambda e, j=j: e.matmul(
                    pC[:], bk_s[0:1, j * 128:(j + 1) * 128],
                    ones_b[0:1, 0:512], start=False, stop=True),
                   [], [("TE", 1)])
                op("SE", lambda e, j=j, st=st: e.copy(
                    KT[:, j * 4096 + st * 512: j * 4096 + (st + 1) * 512],
                    pC[:]),
                   [("TE", tk("TE"))], [("SE", 1)])
                evK = tk("SE")
            for ti in range(4):
                tc = st * 4 + ti
                wv_prev = prev_vst.get("v", 0)
                for dc in range(8):
                    for hh in range(2):
                        op("TE", lambda e, ti=ti, dc=dc, hh=hh: e.matmul(
                            pA[:, hh * 512:(hh + 1) * 512],
                            xTst[:, dc * 512 + ti * 128: dc * 512 + (ti + 1) * 128],
                            wv_s[:, dc * 1024 + hh * 512: dc * 1024 + (hh + 1) * 512],
                            start=(dc == 0), stop=False),
                           [("SE", wv_prev)] if (dc == 0 and hh == 0) else [], [])
                for hh in range(2):
                    op("TE", lambda e, hh=hh: e.matmul(
                        pA[:, hh * 512:(hh + 1) * 512],
                        ones_b[0:1, 0:128], bv_s[0:1, hh * 512:(hh + 1) * 512],
                        start=False, stop=True),
                       [], [("TE", 1)] if hh == 1 else [])
                op("SE", lambda e, tc=tc: e.copy(
                    Vv4[:, tc, :, :],
                    pA.ap().rearrange("p (a b) -> p a b", a=16)),
                   [("TE", tk("TE"))], [("SE", 1)])
                prev_vst["v"] = tk("SE")
            prev_xuse = tk("TE")

        s1_done = barrier()
        op("VE", lambda e: e.memset(sel[:], 0.0), s1_done, [("VE", 1)])

        (MU, SIG, TAUA, ZA, TAUB, ZB, CC, BD, BT, BC, T1, T2, T3,
         AL, RK, TVAL, UB, SLO, S2HI, GE64, TPK, T4, NT) = range(23)

        def sc(q, i=None):
            if i is None:
                return sel[:, 2 * q: 2 * q + 2]
            return sel[:, 2 * q + i: 2 * q + i + 1]

        V = "VE"; SEN = "SE"; T = "TE"

        def vop(fn, waits=(), inc=True):
            op(V, fn, waits, [("VE", 1)] if inc else [])

        def sop(fn, waits=(), inc=True):
            op(SEN, fn, waits, [("SE", 1)] if inc else [])

        for qt in range(NQT):
            if qt > 0:
                barrier()
            dma("sq", xf_t[:], d_xo[qt * 128:(qt + 1) * 128, :],
                waits=[("SE", tk("SE")), ("VE", tk("VE"))])
            dbound("sq")
            dma("gq", xb_t[:], d_xo[qt * 128:(qt + 1) * 128, :],
                waits=[("TE", tk("TE"))])
            ldxb = dbound("gq")
            for j in range(8):
                op(T, lambda e, j=j: e.transpose(
                    pT[:, 0:128], xb_t[:, j * 128:(j + 1) * 128], ident[:]),
                   [("gq", ldxb), ("SE", tk("SE"))], [("TE", 1)])
                sop(lambda e, j=j: e.copy(xT_q[:, j * 128:(j + 1) * 128],
                                          pT[:, 0:128]),
                    [("TE", tk("TE"))])
            xtq = tk("SE")
            evq = 0
            for j in range(8):
                for dc in range(8):
                    op(T, lambda e, j=j, dc=dc: e.matmul(
                        pC[:, 0:128],
                        wq_s[:, dc * 1024 + j * 128: dc * 1024 + (j + 1) * 128],
                        xT_q[:, dc * 128:(dc + 1) * 128],
                        start=(dc == 0), stop=False),
                       [("SE", max(xtq, evq))] if dc == 0 else [], [])
                op(T, lambda e, j=j: e.matmul(
                    pC[:, 0:128], bq_s[0:1, j * 128:(j + 1) * 128],
                    ones_b[0:1, 0:128], start=False, stop=True),
                   [], [("TE", 1)])
                sop(lambda e, j=j: e.copy(
                    Qt[:, j * 128:(j + 1) * 128], pC[:, 0:128]),
                    [("TE", tk("TE"))])
                evq = tk("SE")

            for pj in range(8):
                hA, hB = 2 * pj, 2 * pj + 1
                pair_ve0 = tk("VE")
                pair_te0 = tk("TE")
                first_evac = [True]
                banksAB = [(pC, pD), (pE, pF)]
                ev_tok = [0, 0]
                for kc in range(8):
                    bA, bB = banksAB[kc % 2]
                    w0 = [("SE", ev_tok[kc % 2])] if kc >= 2 else []
                    op(T, lambda e, pj=pj, kc=kc, bA=bA: e.matmul(
                        bA[:], Qt[0:64, pj * 128:(pj + 1) * 128],
                        KT[0:64, pj * 4096 + kc * 512: pj * 4096 + (kc + 1) * 512],
                        start=True, stop=True, tile_position=(0, 0)),
                       w0 + ([("SE", tk("SE"))] if kc < 2 else []), [])
                    op(T, lambda e, pj=pj, kc=kc, bB=bB: e.matmul(
                        bB[:], Qt[64:128, pj * 128:(pj + 1) * 128],
                        KT[64:128, pj * 4096 + kc * 512: pj * 4096 + (kc + 1) * 512],
                        start=True, stop=True, tile_position=(64, 0)),
                       [], [("TE", 1)])
                    tec = tk("TE")
                    for i, bX in ((0, bA), (1, bB)):
                        evw = [("TE", tec)]
                        if first_evac[0]:
                            evw.append(("VE", pair_ve0))
                            first_evac[0] = False
                        sop(lambda e, i=i, kc=kc, bX=bX: e.activation(
                            out=s_sb[:, i * 4096 + kc * 512: i * 4096 + (kc + 1) * 512],
                            in_=bX[:], func=AF.Copy, scale=1.0,
                            accum_out=acc_sb[:, i * 8 + kc: i * 8 + kc + 1]),
                            evw)
                        if kc == 2:
                            sop(lambda e, i=i, bX=bX: e.activation(
                                out=scr4[:, i * 512:(i + 1) * 512], in_=bX[:],
                                func=AF.Square, scale=1.0,
                                accum_out=sq_sb[:, i:i + 1]), [])
                    ev_tok[kc % 2] = tk("SE")
                ph1 = tk("SE")

                vop(lambda e: e.tensor_reduce(
                    sc(MU), acc_sb.ap().rearrange("p (a b) -> p a b", a=2),
                    axis=AX.X, op=ALU.add), [("SE", ph1), ("TE", pair_te0)])
                vop(lambda e: e.tensor_scalar(sc(MU), sc(MU), 1.0 / 4096, None,
                                              op0=ALU.mult))
                vop(lambda e: e.tensor_mul(sc(T1), sc(MU), sc(MU)))
                vop(lambda e: e.scalar_tensor_tensor(
                    sc(T1), sq_sb[:], 1.0 / 512, sc(T1),
                    op0=ALU.mult, op1=ALU.subtract))
                vop(lambda e: e.tensor_scalar(sc(T1), sc(T1), 1e-4, None,
                                              op0=ALU.max))
                sop(lambda e: e.activation(out=sc(T2), in_=sc(T1), func=AF.Ln,
                                           scale=1.0), [("VE", tk("VE"))])
                sop(lambda e: e.activation(out=sc(SIG), in_=sc(T2), func=AF.Exp,
                                           scale=0.5))
                sseed = tk("SE")
                vop(lambda e: e.scalar_tensor_tensor(
                    sc(TAUA), sc(SIG), 2.1535, sc(MU),
                    op0=ALU.mult, op1=ALU.add), [("SE", sseed)])
                vop(lambda e: e.tensor_scalar(sc(SLO), sc(SIG), 0.5, None,
                                              op0=ALU.mult))
                vop(lambda e: e.tensor_scalar(sc(S2HI), sc(SIG), 2.0, None,
                                              op0=ALU.mult))
                vop(lambda e: e.memset(sc(BD), BIG))

                for r in range(RROUNDS):
                    TAU, Z = (TAUA, ZA) if r % 2 == 0 else (TAUB, ZB)
                    TAUP, ZP = (TAUB, ZB) if r % 2 == 0 else (TAUA, ZA)
                    vop(lambda e, TAU=TAU: e.tensor_scalar(
                        sc(NT), sc(TAU), -1.0, None, op0=ALU.mult))
                    veNT = tk("VE")
                    sop(lambda e: e.activation(
                        out=w_sb[:, 4096:8192], in_=s_sb[:, 4096:8192],
                        func=AF.Sign, bias=sc(NT, 1), scale=1.0,
                        accum_out=sq_sb[:, 0:1]), [("VE", veNT)])
                    seCnt = tk("SE")
                    vop(lambda e, TAU=TAU: e.tensor_scalar(
                        w_sb[:, 0:4096], s_sb[:, 0:4096],
                        sc(TAU, 0), None, op0=ALU.is_ge, op1=ALU.add,
                        accum_out=sc(CC, 0)))
                    vop(lambda e: e.tensor_scalar(
                        sc(CC, 1), sq_sb[:, 0:1], 0.5, 2048.0,
                        op0=ALU.mult, op1=ALU.add), [("SE", seCnt)])
                    vop(lambda e: e.tensor_scalar(sc(T1), sc(CC), 64.0, None,
                                                  op0=ALU.subtract))
                    vop(lambda e: e.tensor_mul(sc(T2), sc(T1), sc(T1)))
                    vop(lambda e: e.tensor_tensor(sc(T3), sc(T2), sc(BD),
                                                  op=ALU.is_lt))
                    vop(lambda e: e.tensor_tensor(sc(BD), sc(T2), sc(BD),
                                                  op=ALU.min))
                    vop(lambda e, TAU=TAU: e.tensor_tensor(sc(T4), sc(TAU),
                                                           sc(BT), op=ALU.subtract))
                    vop(lambda e: e.tensor_mul(sc(T4), sc(T4), sc(T3)))
                    vop(lambda e: e.tensor_add(sc(BT), sc(BT), sc(T4)))
                    vop(lambda e: e.tensor_tensor(sc(T4), sc(CC), sc(BC),
                                                  op=ALU.subtract))
                    vop(lambda e: e.tensor_mul(sc(T4), sc(T4), sc(T3)))
                    vop(lambda e: e.tensor_add(sc(BC), sc(BC), sc(T4)))
                    if r == RROUNDS - 1:
                        break
                    vop(lambda e: e.tensor_scalar(sc(T1), sc(CC), 0.7, 1300.0,
                                                  op0=ALU.max, op1=ALU.min))
                    sop(lambda e: e.activation(out=sc(T2), in_=sc(T1),
                                               func=AF.Ln, scale=1.0),
                        [("VE", tk("VE"))])
                    lnr = tk("SE")
                    vop(lambda e: e.tensor_scalar(sc(T1), sc(T2), ZC3, ZC2,
                                                  op0=ALU.mult, op1=ALU.add),
                        [("SE", lnr)])
                    vop(lambda e: e.tensor_mul(sc(T1), sc(T1), sc(T2)))
                    vop(lambda e, Z=Z: e.scalar_tensor_tensor(
                        sc(Z), sc(T1), ZC1, sc(T2), op0=ALU.add, op1=ALU.mult))
                    if r == 0:
                        vop(lambda e, Z=Z: e.tensor_scalar(
                            sc(T1), sc(Z), ZC0, None, op0=ALU.add))
                        vop(lambda e: e.reciprocal(sc(T2), sc(T1)))
                        vop(lambda e: e.tensor_mul(sc(T2), sc(T2), sc(SIG)))
                        vop(lambda e: e.tensor_scalar(sc(T2), sc(T2), 2.1535,
                                                      None, op0=ALU.mult))
                    else:
                        vop(lambda e, Z=Z, ZP=ZP: e.tensor_tensor(
                            sc(T1), sc(Z), sc(ZP), op=ALU.subtract))
                        vop(lambda e: e.tensor_mul(sc(T2), sc(T1), sc(T1)))
                        vop(lambda e: e.tensor_scalar(sc(T2), sc(T2), 1e-8,
                                                      None, op0=ALU.add))
                        vop(lambda e: e.reciprocal(sc(T4), sc(T2)))
                        vop(lambda e: e.tensor_mul(sc(T1), sc(T1), sc(T4)))
                        vop(lambda e, TAU=TAU, TAUP=TAUP: e.tensor_tensor(
                            sc(T2), sc(TAU), sc(TAUP), op=ALU.subtract))
                        vop(lambda e: e.tensor_mul(sc(T2), sc(T2), sc(T1)))
                    vop(lambda e: e.tensor_tensor(sc(T2), sc(T2), sc(S2HI),
                                                  op=ALU.min))
                    vop(lambda e: e.tensor_tensor(sc(T2), sc(T2), sc(SLO),
                                                  op=ALU.max))
                    vop(lambda e, Z=Z: e.tensor_scalar(
                        sc(T1), sc(Z), Z64T, None, op0=ALU.subtract))
                    vop(lambda e: e.tensor_mul(sc(T1), sc(T1), sc(T2)))
                    vop(lambda e, TAU=TAU, TAUP=TAUP: e.tensor_tensor(
                        sc(TAUP), sc(TAU), sc(T1), op=ALU.subtract))

                vop(lambda e: e.tensor_scalar(sc(GE64), sc(BC), 64.0, None,
                                              op0=ALU.is_ge))
                vop(lambda e: e.tensor_scalar(sc(AL), sc(GE64), -2.0, 1.0,
                                              op0=ALU.mult, op1=ALU.add))
                vop(lambda e: e.tensor_scalar(sc(T1), sc(BC), 64.0, None,
                                              op0=ALU.subtract))
                vop(lambda e: e.tensor_scalar(sc(T2), sc(BC), -1.0, 63.0,
                                              op0=ALU.mult, op1=ALU.add))
                vop(lambda e: e.tensor_tensor(sc(T4), sc(T1), sc(T2),
                                              op=ALU.subtract))
                vop(lambda e: e.tensor_mul(sc(T4), sc(T4), sc(GE64)))
                vop(lambda e: e.tensor_add(sc(RK), sc(T2), sc(T4)))
                vop(lambda e: e.tensor_scalar(sc(RK), sc(RK), 0.0, 7.0,
                                              op0=ALU.max, op1=ALU.min))
                vop(lambda e: e.tensor_mul(sc(UB), sc(AL), sc(BT)))
                for i in range(2):
                    vop(lambda e, i=i: e.tensor_scalar(
                        w_sb[:, 4096:8192], s_sb[:, i * 4096:(i + 1) * 4096],
                        sc(AL, i), None, op0=ALU.mult))
                    vop(lambda e, i=i: e.tensor_scalar(
                        w_sb[:, 0:4096], w_sb[:, 4096:8192], sc(UB, i), BIG,
                        op0=ALU.is_ge, op1=ALU.mult))
                    vop(lambda e: e.tensor_sub(w_sb[:, 4096:8192],
                                               w_sb[:, 4096:8192],
                                               w_sb[:, 0:4096]))
                    vop(lambda e, i=i: e.max(out=d8_sb[:, i * 8:(i + 1) * 8],
                                             in_=w_sb[:, 4096:8192]))
                    vop(lambda e, i=i: e.tensor_copy(
                        rt_t[:, 8:16], d8_sb[:, i * 8:(i + 1) * 8]))
                    vop(lambda e, i=i: e.tensor_scalar(
                        rt_t[:, 0:8], iota_s[:, 0:8], sc(RK, i), None,
                        op0=ALU.is_equal))
                    vop(lambda e: e.tensor_mul(rt_t[:, 16:24], rt_t[:, 0:8],
                                               rt_t[:, 8:16]))
                    vop(lambda e, i=i: e.reduce_sum(sc(TPK, i), rt_t[:, 16:24],
                                                    axis=AX.X))
                vop(lambda e: e.tensor_mul(sc(TVAL), sc(TPK), sc(AL)))

                for i, h in ((0, hA), (1, hB)):
                    wreg = w_sb[:, i * 4096:(i + 1) * 4096]
                    wtreg = s_sb[:, i * 4096:(i + 1) * 4096]
                    vop(lambda e, i=i: e.max(out=d8_sb[:, i * 8:(i + 1) * 8],
                                             in_=s_sb[:, i * 4096:(i + 1) * 4096]))
                    vop(lambda e, i=i: e.tensor_copy(
                        sc(T1, i), d8_sb[:, i * 8: i * 8 + 1]))
                    vop(lambda e, i=i: e.tensor_scalar(
                        sc(T2, i), sc(T1, i), -650.0, None, op0=ALU.add))
                    vop(lambda e, i=i: e.tensor_tensor(
                        sc(T2, i), sc(T2, i), sc(TVAL, i), op=ALU.max))
                    vop(lambda e, i=i: e.tensor_scalar(
                        sc(T3, i), sc(T2, i), -0.125, None, op0=ALU.mult))
                    veSH = tk("VE")
                    sop(lambda e, i=i, wreg=wreg: e.activation(
                        out=wreg, in_=s_sb[:, i * 4096:(i + 1) * 4096],
                        func=AF.Exp, scale=0.125, bias=sc(T3, i)),
                        [("VE", veSH)])
                    seE = tk("SE")
                    vop(lambda e, i=i, wreg=wreg: e.scalar_tensor_tensor(
                        wreg, s_sb[:, i * 4096:(i + 1) * 4096], sc(TVAL, i),
                        wreg, op0=ALU.is_ge, op1=ALU.mult,
                        accum_out=sc(T4, i)), [("SE", seE)])
                    vop(lambda e, i=i: e.reciprocal(sc(T1, i), sc(T4, i)))
                    vop(lambda e, i=i, wreg=wreg: e.tensor_scalar(
                        wreg, wreg, sc(T1, i), None, op0=ALU.mult))
                    wnorm = tk("VE")
                    ev_tok2 = [0, 0]
                    wv_done = 0
                    for kc2 in range(32):
                        sl = kc2 % 2
                        op(T, lambda e, kc2=kc2, sl=sl, wreg=wreg: e.transpose(
                            pT[:, sl * 128:(sl + 1) * 128],
                            wreg[:, kc2 * 128:(kc2 + 1) * 128], ident[:]),
                           [("VE", wnorm), ("SE", ev_tok2[sl])], [("TE", 1)])
                        trk = tk("TE")
                        sop(lambda e, kc2=kc2, sl=sl, wtreg=wtreg: e.copy(
                            wtreg[:, kc2 * 128:(kc2 + 1) * 128],
                            pT[:, sl * 128:(sl + 1) * 128]),
                            [("TE", trk)])
                        ev_tok2[sl] = tk("SE")
                        op(T, lambda e, kc2=kc2, h=h, wtreg=wtreg: e.matmul(
                            pG[0:64, 0:128], Vv4[:, kc2, h, 0:64],
                            wtreg[:, kc2 * 128:(kc2 + 1) * 128],
                            start=(kc2 == 0), stop=(kc2 == 31)),
                           [("SE", ev_tok2[sl])],
                           [("TE", 1)] if kc2 == 31 else [])
                    teW = tk("TE")
                    sop(lambda e, h=h: e.copy(
                        attnT[64 * (h % 2):64 * (h % 2) + 64,
                              (h // 2) * 128:(h // 2) * 128 + 128],
                        pG[0:64, 0:128]), [("TE", teW)])

            attn_done = tk("SE")
            opj_prev = 0
            for j in range(8):
                dma("gq", wo_sl[:], d_woT[j * 128:(j + 1) * 128, :],
                    waits=[("TE", opj_prev)] +
                          ([("SE", attn_done), ("VE", tk("VE"))] if j == 0 else []))
                ldw = dbound("gq")
                for hh in range(2):
                    op(T, lambda e, j=j, hh=hh: e.matmul(
                        pA[:, hh * 512:(hh + 1) * 512],
                        attnT[:, j * 128:(j + 1) * 128],
                        wo_sl[:, hh * 512:(hh + 1) * 512],
                        start=(j == 0), stop=False),
                       ([("gq", ldw), ("SE", attn_done)] +
                        ([("VE", tk("VE"))] if j == 0 else []))
                       if hh == 0 else [],
                       [("TE", 1)] if hh == 1 else [])
                opj_prev = tk("TE")
            for hh in range(2):
                op(T, lambda e, hh=hh: e.matmul(
                    pA[:, hh * 512:(hh + 1) * 512],
                    ones_b[0:1, 0:128], bo_s[0:1, hh * 512:(hh + 1) * 512],
                    start=False, stop=True), [], [("TE", 1)] if hh == 1 else [])
            teO = tk("TE")
            vop(lambda e: e.scalar_tensor_tensor(
                r_sb[:], xf_t[:], 2.0, pA.ap(),
                op0=ALU.mult, op1=ALU.add, accum_out=lnst[:, 0:1]),
                [("TE", teO), ("sq", dbound("sq"))])
            veR = tk("VE")
            sop(lambda e: e.activation(out=scr4[:], in_=r_sb[:], func=AF.Square,
                                       scale=1.0, accum_out=lnst[:, 1:2]),
                [("VE", veR)])
            seR = tk("SE")
            vop(lambda e: e.tensor_scalar(lnst[:, 2:3], lnst[:, 0:1],
                                          1.0 / 1024, None, op0=ALU.mult),
                [("SE", seR)])
            vop(lambda e: e.tensor_mul(lnst[:, 3:4], lnst[:, 2:3], lnst[:, 2:3]))
            vop(lambda e: e.scalar_tensor_tensor(
                lnst[:, 3:4], lnst[:, 1:2], 1.0 / 1024, lnst[:, 3:4],
                op0=ALU.mult, op1=ALU.subtract))
            vop(lambda e: e.tensor_scalar(lnst[:, 3:4], lnst[:, 3:4], LN_EPS,
                                          None, op0=ALU.add))
            sop(lambda e: e.activation(out=lnst[:, 4:5], in_=lnst[:, 3:4],
                                       func=AF.Ln, scale=1.0),
                [("VE", tk("VE"))])
            sop(lambda e: e.activation(out=lnst[:, 5:6], in_=lnst[:, 4:5],
                                       func=AF.Exp, scale=-0.5))
            seV = tk("SE")
            vop(lambda e: e.tensor_scalar(scr4[:], r_sb[:], lnst[:, 2:3],
                                          lnst[:, 5:6], op0=ALU.subtract,
                                          op1=ALU.mult), [("SE", seV)])
            vop(lambda e: e.tensor_scalar(xb_t[:], r_sb[:], lnst[:, 2:3],
                                          lnst[:, 5:6], op0=ALU.subtract,
                                          op1=ALU.mult))
            seX1 = tk("VE")
            dma("sq", d_x1f[qt * 128:(qt + 1) * 128, :], scr4[:],
                waits=[("VE", seX1)])
            for j in range(8):
                op(T, lambda e, j=j: e.transpose(
                    pT[:, 0:128], xb_t[:, j * 128:(j + 1) * 128], ident[:]),
                   ([("VE", seX1)] if j == 0 else []) + [("SE", tk("SE"))],
                   [("TE", 1)])
                sop(lambda e, j=j: e.copy(xT_q[:, j * 128:(j + 1) * 128],
                                          pT[:, 0:128]),
                    [("TE", tk("TE"))])
            seXT = tk("SE")
            for j in range(8):
                dma("sq", d_x1bT[j * 128:(j + 1) * 128,
                                 qt * 128:(qt + 1) * 128],
                    xT_q[:, j * 128:(j + 1) * 128],
                    waits=[("SE", seXT)] if j == 0 else [])

            for dc in range(8):
                op(T, lambda e, dc=dc: e.matmul(
                    pC[:, 0:8], xT_q[:, dc * 128:(dc + 1) * 128],
                    rw_s[:, dc * 8:(dc + 1) * 8],
                    start=(dc == 0), stop=(dc == 7)),
                   [("SE", seXT), ("VE", tk("VE"))] if dc == 0 else [],
                   [("TE", 1)] if dc == 7 else [])
            teR = tk("TE")
            vop(lambda e: e.reduce_max(rt_t[:, 8:9], pC[:, 0:8], axis=AX.X),
                [("TE", teR)])
            vop(lambda e: e.tensor_scalar(rt_t[:, 8:9], rt_t[:, 8:9], -1.0,
                                          None, op0=ALU.mult))
            veM = tk("VE")
            sop(lambda e: e.activation(out=rt_t[:, 0:8], in_=pC[:, 0:8],
                                       func=AF.Exp, bias=rt_t[:, 8:9],
                                       scale=1.0, accum_out=rt_t[:, 9:10]),
                [("VE", veM)])
            seP = tk("SE")
            vop(lambda e: e.reciprocal(rt_t[:, 10:11], rt_t[:, 9:10]),
                [("SE", seP)])
            vop(lambda e: e.tensor_scalar(rt_t[:, 0:8], rt_t[:, 0:8],
                                          rt_t[:, 10:11], None, op0=ALU.mult))
            probs_done = tk("VE")

            for dc in range(8):
                op(T, lambda e, dc=dc: e.matmul(
                    pD[0:AHID, 0:128], a1_s[:, dc * 64: dc * 64 + AHID],
                    xT_q[:, dc * 128:(dc + 1) * 128],
                    start=(dc == 0), stop=False),
                   [("TE", teR)] if dc == 0 else [], [])
            op(T, lambda e: e.matmul(
                pD[0:AHID, 0:128], ab1_s[0:1, 0:AHID], ones_b[0:1, 0:128],
                start=False, stop=True), [], [("TE", 1)])
            teA = tk("TE")
            sop(lambda e: e.copy(ac_f[:], pD[0:AHID, 0:128]), [("TE", teA)])
            seA1 = tk("SE")
            sop(lambda e: e.activation(out=ac_g[:], in_=ac_f[:],
                                       func=AF.Square, scale=1.0))
            seA2 = tk("SE")
            vop(lambda e: e.tensor_mul(ac_g[:], ac_g[:], ac_f[:]),
                [("SE", seA2)])
            vop(lambda e: e.scalar_tensor_tensor(
                ac_g[:], ac_g[:], 0.044715, ac_f[:],
                op0=ALU.mult, op1=ALU.add))
            veA = tk("VE")
            sop(lambda e: e.activation(out=ac_g[:], in_=ac_g[:], func=AF.Tanh,
                                       scale=GELU_C), [("VE", veA)])
            seA3 = tk("SE")
            vop(lambda e: e.scalar_tensor_tensor(
                ac_g[:], ac_g[:], 1.0, ac_f[:], op0=ALU.add, op1=ALU.mult),
                [("SE", seA3)])
            vop(lambda e: e.tensor_scalar(ac_b[:], ac_g[:], 0.5, None,
                                          op0=ALU.mult))
            veG = tk("VE")
            op(T, lambda e: e.matmul(pC[0:MAXK, 128:256], a2_s[:], ac_b[:],
                                     start=True, stop=False),
               [("VE", veG)], [])
            op(T, lambda e: e.matmul(
                pC[0:MAXK, 128:256], ab2_s[0:1, 0:MAXK], ones_b[0:1, 0:128],
                start=False, stop=True), [], [("TE", 1)])
            teL = tk("TE")
            sop(lambda e: e.copy(ac_f[0:MAXK, 0:128], pC[0:MAXK, 128:256]),
                [("TE", teL)])
            seL = tk("SE")
            vop(lambda e: e.tensor_scalar(ac_b[0:MAXK, 0:128],
                                          ac_f[0:MAXK, 0:128], 30.0, -30.0,
                                          op0=ALU.min, op1=ALU.max),
                [("SE", seL)])
            veL = tk("VE")
            op(T, lambda e: e.transpose(pT[0:128, 0:MAXK],
                                        ac_b[0:MAXK, 0:128], ident[0:MAXK, 0:MAXK]),
               [("VE", veL), ("SE", tk("SE"))], [("TE", 1)])
            sop(lambda e: e.copy(lt_f[:], pT[0:128, 0:MAXK]),
                [("TE", tk("TE"))])
            seLT = tk("SE")
            vop(lambda e: e.reduce_max(rt_t[:, 11:12], lt_f[:], axis=AX.X),
                [("SE", seLT)])
            vop(lambda e: e.tensor_scalar(rt_t[:, 16:22], lt_f[:],
                                          rt_t[:, 11:12], BIG,
                                          op0=ALU.is_lt, op1=ALU.mult))
            vop(lambda e: e.tensor_add(rt_t[:, 16:22], rt_t[:, 16:22],
                                       iota_s[:, 0:MAXK]))
            vop(lambda e: e.tensor_reduce(rt_t[:, 12:13], rt_t[:, 16:22],
                                          axis=AX.X, op=ALU.min))
            vop(lambda e: e.memset(rt_t[:, 16:24], 0.0))
            for ee in range(8):
                vop(lambda e, ee=ee: e.tensor_scalar(
                    acc_sb[:, 0:8], rt_t[:, 0:8], rt_t[:, ee:ee + 1], None,
                    op0=ALU.is_gt))
                vop(lambda e: e.tensor_add(rt_t[:, 16:24], rt_t[:, 16:24],
                                           acc_sb[:, 0:8]))
            vop(lambda e: e.tensor_scalar(rt_t[:, 16:24], rt_t[:, 16:24],
                                          -1.0, 7.0, op0=ALU.mult,
                                          op1=ALU.add))
            vop(lambda e: e.tensor_scalar(rt_t[:, 16:24], rt_t[:, 16:24],
                                          rt_t[:, 12:13], None, op0=ALU.is_le))
            vop(lambda e, qt=qt: e.tensor_tensor(
                gsc[:, qt * 8:(qt + 1) * 8], rt_t[:, 16:24], rt_t[:, 0:8],
                op=ALU.mult))

        s2a_done = barrier()

        for dc in range(8):
            dma("sq", x1T_s[:, dc * NTOK:(dc + 1) * NTOK],
                d_x1bT[dc * 128:(dc + 1) * 128, :], waits=s2a_done if dc == 0 else [])
        x1T_ld = dbound("sq")
        facc_tok = [0] * NQT
        gu_prev = 0
        fst_prev = 0
        exp_te = 0
        for e_i in range(NEXP):
            for dc in range(8):
                dma("gq", gate_s[:, dc * IFFN:(dc + 1) * IFFN],
                    d_gT[e_i, dc * 128:(dc + 1) * 128, :],
                    waits=([("TE", exp_te)] if dc == 0 else []))
                dma("gq", up_s[:, dc * IFFN:(dc + 1) * IFFN],
                    d_uT[e_i, dc * 128:(dc + 1) * 128, :])
            w_ld = dbound("gq")
            dn_prev = [0, 0]
            dn_evac = 0
            for tb in range(8):
                for ic in range(NIC):
                    dbuf = ic % 2
                    dma("gq", down_s[:, dbuf * 1024:(dbuf + 1) * 1024],
                        d_dT[e_i, ic * 128:(ic + 1) * 128, :],
                        waits=[("TE", dn_prev[dbuf])])
                    d_ld = dbound("gq")
                    for dc in range(8):
                        op(T, lambda e, e_i=e_i, tb=tb, ic=ic, dc=dc: e.matmul(
                            pC[:, 0:256],
                            gate_s[:, dc * IFFN + ic * 128: dc * IFFN + (ic + 1) * 128],
                            x1T_s[:, dc * NTOK + tb * 256: dc * NTOK + (tb + 1) * 256],
                            start=(dc == 0), stop=(dc == 7)),
                           ([("gq", w_ld), ("sq", x1T_ld), ("SE", tk("SE"))]
                            if dc == 0 else []), [])
                        op(T, lambda e, e_i=e_i, tb=tb, ic=ic, dc=dc: e.matmul(
                            pD[:, 0:256],
                            up_s[:, dc * IFFN + ic * 128: dc * IFFN + (ic + 1) * 128],
                            x1T_s[:, dc * NTOK + tb * 256: dc * NTOK + (tb + 1) * 256],
                            start=(dc == 0), stop=(dc == 7)),
                           [("VE", gu_prev)] if dc == 0 else [],
                           [("TE", 1)] if dc == 7 else [])
                    teGU = tk("TE")
                    sop(lambda e: e.activation(out=sg_s[:], in_=pC[:, 0:256],
                                               func=AF.Silu, scale=1.0),
                        [("TE", teGU)])
                    seS = tk("SE")
                    vop(lambda e: e.tensor_tensor(gu_s[:], sg_s[:],
                                                  pD[:, 0:256], op=ALU.mult),
                        [("SE", seS), ("TE", dn_prev[1 - dbuf])])
                    gu_prev = tk("VE")
                    for q4 in range(2):
                        for hh in range(2):
                            op(T, lambda e, ic=ic, q4=q4, hh=hh, dbuf=dbuf: e.matmul(
                                (pA[:, hh * 512:(hh + 1) * 512]) if q4 == 0 else
                                (pG[:, 0:512] if hh == 0 else pF[:, 0:512]),
                                gu_s[:, q4 * 128:(q4 + 1) * 128],
                                down_s[:, dbuf * 1024 + hh * 512: dbuf * 1024 + (hh + 1) * 512],
                                start=(ic == 0), stop=(ic == NIC - 1)),
                               ([("VE", gu_prev), ("gq", d_ld)] +
                                ([("SE", dn_evac)] if ic == 0 else []))
                               if (q4 == 0 and hh == 0) else [],
                               [("TE", 1)] if (q4 == 1 and hh == 1) else [])
                    dn_prev[dbuf] = tk("TE")
                teD = tk("TE")
                for q4 in range(2):
                    qtt = tb * 2 + q4
                    srcs = [pA.ap()] if q4 == 0 else [pG.ap(), pF.ap()]
                    if q4 == 0:
                        vop(lambda e, e_i=e_i, qtt=qtt: e.tensor_scalar(
                            fst_s[:], pA[:],
                            gsc[:, qtt * 8 + e_i: qtt * 8 + e_i + 1], None,
                            op0=ALU.mult),
                            [("TE", teD), ("gq", max(facc_tok[qtt], fst_prev))])
                    else:
                        vop(lambda e, e_i=e_i, qtt=qtt: e.tensor_scalar(
                            fst_s[:, 0:512], pG[:, 0:512],
                            gsc[:, qtt * 8 + e_i: qtt * 8 + e_i + 1], None,
                            op0=ALU.mult),
                            [("TE", teD), ("gq", max(facc_tok[qtt], fst_prev))])
                        vop(lambda e, e_i=e_i, qtt=qtt: e.tensor_scalar(
                            fst_s[:, 512:1024], pF[:, 0:512],
                            gsc[:, qtt * 8 + e_i: qtt * 8 + e_i + 1], None,
                            op0=ALU.mult), [])
                    seF = tk("VE")
                    fw = [("VE", seF)]
                    if fence_req["gq"] > fence_app["gq"]:
                        fw.append(("gq", fence_req["gq"]))
                        fence_app["gq"] = fence_req["gq"]
                    op("gq", lambda e, e_i=e_i, qtt=qtt: e.dma_start(
                        out=d_facc[qtt * 128:(qtt + 1) * 128, :], in_=fst_s[:],
                        accum_op=(ALU.bypass if e_i == 0 else ALU.add)),
                       fw, [("gq", 16)])
                    facc_tok[qtt] = dbound("gq")
                    fst_prev = facc_tok[qtt]
            exp_te = tk("TE")

        s2b_done = barrier()

        prev_ln = 0
        for qt in range(NQT):
            dma("sq", xf_t[:], d_x1f[qt * 128:(qt + 1) * 128, :],
                waits=(s2b_done if qt == 0 else []) + [("SE", prev_ln),
                                                       ("VE", tk("VE"))])
            dma("sq", r_sb[:], d_facc[qt * 128:(qt + 1) * 128, :])
            ld2 = dbound("sq")
            vop(lambda e: e.scalar_tensor_tensor(
                scr4[:], xf_t[:], 1.0, r_sb[:], op0=ALU.mult, op1=ALU.add,
                accum_out=lnst[:, 0:1]), [("sq", ld2), ("SE", prev_ln)])
            veR = tk("VE")
            sop(lambda e: e.activation(out=r_sb[:], in_=scr4[:], func=AF.Square,
                                       scale=1.0, accum_out=lnst[:, 1:2]),
                [("VE", veR)])
            seR = tk("SE")
            vop(lambda e: e.tensor_scalar(lnst[:, 2:3], lnst[:, 0:1],
                                          1.0 / 1024, None, op0=ALU.mult),
                [("SE", seR)])
            vop(lambda e: e.tensor_mul(lnst[:, 3:4], lnst[:, 2:3], lnst[:, 2:3]))
            vop(lambda e: e.scalar_tensor_tensor(
                lnst[:, 3:4], lnst[:, 1:2], 1.0 / 1024, lnst[:, 3:4],
                op0=ALU.mult, op1=ALU.subtract))
            vop(lambda e: e.tensor_scalar(lnst[:, 3:4], lnst[:, 3:4], LN_EPS,
                                          None, op0=ALU.add))
            sop(lambda e: e.activation(out=lnst[:, 4:5], in_=lnst[:, 3:4],
                                       func=AF.Ln, scale=1.0),
                [("VE", tk("VE"))])
            sop(lambda e: e.activation(out=lnst[:, 5:6], in_=lnst[:, 4:5],
                                       func=AF.Exp, scale=-0.5))
            seV = tk("SE")
            vop(lambda e: e.tensor_scalar(xf_t[:], scr4[:], lnst[:, 2:3],
                                          lnst[:, 5:6], op0=ALU.subtract,
                                          op1=ALU.mult), [("SE", seV)])
            prev_ln2 = tk("VE")
            dma("sq", d_y[qt * 128:(qt + 1) * 128, :], xf_t[:],
                waits=[("VE", prev_ln2)])
            dbound("sq")

        op("GE", lambda e: e.memset(ones_b[0:1, 0:1], 1.0),
           [("sq", tk("sq")), ("gq", tk("gq")), ("SE", tk("SE")),
            ("VE", tk("VE")), ("TE", tk("TE"))], [("GE", 1)])

        def stream(names):
            def body(eng):
                for engname, fn, waits, incs in ops:
                    if engname not in names:
                        continue
                    for s, v in waits:
                        eng.wait_ge(sems[s], v)
                    instr = fn(eng)
                    for s, n in incs:
                        instr.then_inc(sems[s], n)
            return body

        block.sync(stream(("sq",)))
        block.tensor(stream(("TE",)))
        block.vector(stream(("VE",)))
        block.scalar(stream(("SE",)))
        block.gpsimd(stream(("gq", "GE")))

    return nc


def _stage_inputs(inputs):
    f = np.float32
    qkv_w = np.asarray(inputs["L0_attn_w_qkv_w"], f).reshape(H, 3, DH, D)
    qkv_b = np.asarray(inputs["L0_attn_w_qkv_b"], f).reshape(H, 3, DH)
    wq = np.ascontiguousarray(qkv_w[:, 0].reshape(D, D).T)
    wk = np.ascontiguousarray(qkv_w[:, 1].reshape(D, D).T)
    wv = np.ascontiguousarray(qkv_w[:, 2].reshape(D, D).T)
    bq = np.ascontiguousarray(qkv_b[:, 0].reshape(1, D))
    bk = np.ascontiguousarray(qkv_b[:, 1].reshape(1, D))
    bv = np.ascontiguousarray(qkv_b[:, 2].reshape(1, D))
    woT = np.ascontiguousarray(np.asarray(inputs["L0_attn_w_out_w"], f).T)
    bo = np.ascontiguousarray(np.asarray(inputs["L0_attn_w_out_b"], f).reshape(1, D))
    rwT = np.ascontiguousarray(np.asarray(inputs["L0_ffn_router_w"], f).T)
    a1T = np.ascontiguousarray(np.asarray(inputs["L0_ffn_actor_w1"], f).T)
    ab1 = np.ascontiguousarray(np.asarray(inputs["L0_ffn_actor_b1"], f).reshape(1, AHID))
    a2T = np.ascontiguousarray(np.asarray(inputs["L0_ffn_actor_w2"], f).T)
    ab2 = np.ascontiguousarray(np.asarray(inputs["L0_ffn_actor_b2"], f).reshape(1, MAXK))
    gT = np.ascontiguousarray(np.asarray(inputs["L0_ffn_gate_w"], f).transpose(0, 2, 1))
    uT = np.ascontiguousarray(np.asarray(inputs["L0_ffn_up_w"], f).transpose(0, 2, 1))
    dT = np.ascontiguousarray(np.asarray(inputs["L0_ffn_down_w"], f).transpose(0, 2, 1))
    iota = np.broadcast_to(np.arange(16, dtype=f), (128, 16)).copy()
    ident = np.eye(128, dtype=f).astype(nbf)
    com = dict(wqT=wq, wkT=wk, wvT=wv, bq=bq, bk=bk, bv=bv, woT=woT, bo=bo,
               rwT=rwT, a1T=a1T, ab1=ab1, a2T=a2T, ab2=ab2,
               gT=gT, uT=uT, dT=dT, iota=iota, ident=ident)
    x = np.asarray(inputs["L0_attn_x"], f)
    in_maps = []
    for c in range(8):
        b, hf = c // 2, c % 2
        m = dict(com)
        m["xseq"] = np.ascontiguousarray(x[b])
        m["xown"] = np.ascontiguousarray(x[b, hf * NTOK:(hf + 1) * NTOK])
        in_maps.append(m)
    return in_maps


def kernel(**inputs):
    in_maps = _stage_inputs(inputs)
    nc = build_nc()
    res = run_bass_kernel_spmd(nc, in_maps, core_ids=list(range(8)))
    y = np.empty((B, S, D), np.float32)
    for c in range(8):
        b, hf = c // 2, c % 2
        y[b, hf * NTOK:(hf + 1) * NTOK] = res.results[c]["y"]
    return y


# version 08 (final)
# speedup vs baseline: 1.0799x; 1.0002x over previous
import numpy as np
import ml_dtypes
from contextlib import ExitStack
import concourse.bass as bass
import concourse.mybir as mybir
from concourse.bass_utils import run_bass_kernel_spmd

F32 = mybir.dt.float32
BF16 = mybir.dt.bfloat16
AF = mybir.ActivationFunctionType
ALU = mybir.AluOpType
AX = mybir.AxisListType
nbf = ml_dtypes.bfloat16

B, S, D = 4, 4096, 1024
H, DH = 16, 64
NTOK = 2048
NQT = NTOK // 128
NST = S // 512
IFFN, NEXP, AHID, MAXK = 2688, 8, 50, 6
NIC = IFFN // 128

RROUNDS = 7
ZC0, ZC1, ZC2, ZC3 = 3.4934115949635056, -0.2996396124498922, \
    0.01051770243145956, -0.0037008912809003325
Z64T = 2.162945490949954 - ZC0
BIG = 1.0e30
LN_EPS = 1.0e-5
GELU_C = 0.7978845608028654


def build_nc():
    nc = bass.Bass("TRN2", target_bir_lowering=False)

    d_xs = nc.dram_tensor("xseq", [S, D], F32, kind="ExternalInput").ap()
    d_xo = nc.dram_tensor("xown", [NTOK, D], F32, kind="ExternalInput").ap()
    d_wqT = nc.dram_tensor("wqT", [D, D], F32, kind="ExternalInput").ap()
    d_wkT = nc.dram_tensor("wkT", [D, D], F32, kind="ExternalInput").ap()
    d_wvT = nc.dram_tensor("wvT", [D, D], F32, kind="ExternalInput").ap()
    d_bq = nc.dram_tensor("bq", [1, D], F32, kind="ExternalInput").ap()
    d_bk = nc.dram_tensor("bk", [1, D], F32, kind="ExternalInput").ap()
    d_bv = nc.dram_tensor("bv", [1, D], F32, kind="ExternalInput").ap()
    d_woT = nc.dram_tensor("woT", [D, D], F32, kind="ExternalInput").ap()
    d_bo = nc.dram_tensor("bo", [1, D], F32, kind="ExternalInput").ap()
    d_rwT = nc.dram_tensor("rwT", [D, 8], F32, kind="ExternalInput").ap()
    d_a1T = nc.dram_tensor("a1T", [D, AHID], F32, kind="ExternalInput").ap()
    d_ab1 = nc.dram_tensor("ab1", [1, AHID], F32, kind="ExternalInput").ap()
    d_a2T = nc.dram_tensor("a2T", [AHID, MAXK], F32, kind="ExternalInput").ap()
    d_ab2 = nc.dram_tensor("ab2", [1, MAXK], F32, kind="ExternalInput").ap()
    d_gT = nc.dram_tensor("gT", [NEXP, D, IFFN], F32, kind="ExternalInput").ap()
    d_uT = nc.dram_tensor("uT", [NEXP, D, IFFN], F32, kind="ExternalInput").ap()
    d_dT = nc.dram_tensor("dT", [NEXP, IFFN, D], F32, kind="ExternalInput").ap()
    d_iota = nc.dram_tensor("iota", [128, 16], F32, kind="ExternalInput").ap()
    d_ident = nc.dram_tensor("ident", [128, 128], BF16, kind="ExternalInput").ap()
    d_x1f = nc.dram_tensor("x1f", [NTOK, D], F32).ap()
    d_x1bT = nc.dram_tensor("x1bT", [D, NTOK], BF16).ap()
    d_facc = nc.dram_tensor("facc", [NTOK, D], F32).ap()
    d_y = nc.dram_tensor("y", [NTOK, D], F32, kind="ExternalOutput").ap()

    def at(name, shape, dtype, offset):
        return nc.alloc_sbuf_tensor_at(name, shape, dtype, offset=offset)

    SM = 18432
    iota_s = at("iota_s", [128, 16], F32, SM + 0)
    ident = at("ident_s", [128, 128], BF16, SM + 64)
    ones_b = at("ones_b", [1, 512], BF16, SM + 320)
    bq_s = at("bq_s", [1, D], BF16, SM + 1344)
    bo_s = at("bo_s", [1, D], BF16, SM + 3392)
    gsc = at("gsc", [128, 8 * NQT], F32, SM + 5440)
    bk_s = at("bk_s", [1, D], BF16, SM + 5952)
    bv_s = at("bv_s", [1, D], BF16, SM + 8000)
    ab1_s = at("ab1_s", [1, 64], BF16, SM + 10048)
    ab2_s = at("ab2_s", [1, 8], BF16, SM + 10176)
    a2_s = at("a2_s", [AHID, MAXK], BF16, SM + 10208)
    sel = at("sel", [128, 48], F32, SM + 10240)
    acc_sb = at("acc_sb", [128, 16], F32, SM + 10432)
    sq_sb = at("sq_sb", [128, 2], F32, SM + 10496)
    lnst = at("lnst", [128, 8], F32, SM + 10528)
    d8_sb = at("d8_sb", [128, 16], BF16, SM + 10560)
    rt_t = at("rt_t", [128, 24], F32, SM + 10592)
    lt_f = at("lt_f", [128, MAXK], F32, SM + 10688)
    gu_s = at("gu_s", [128, 256], BF16, SM + 10720)
    sg_s = at("sg_s", [128, 256], BF16, SM + 11232)
    rw_s = at("rw_s", [128, 8 * 8], BF16, SM + 11744)
    a1_s = at("a1_s", [128, 8 * 64], BF16, SM + 11872)
    ac_f = at("ac_f", [AHID, 128], F32, SM + 12896)
    ac_g = at("ac_g", [AHID, 128], F32, SM + 13408)
    ac_b = at("ac_b", [AHID, 128], BF16, SM + 13920)
    BG = 32704
    KT = at("KT", [128, 8 * 4096], BF16, BG)
    Vv = at("Vv", [128, 32 * 1024], BF16, BG + 65536)
    wq_s = at("wq_s", [128, 8 * 1024], BF16, BG + 131072)
    AR = BG + 147456

    wk_s = at("wk_s", [128, 8 * 1024], BF16, AR)
    wv_s = at("wv_s", [128, 8 * 1024], BF16, AR + 16384)
    xbst = at("xbst", [128, 4 * 1024], BF16, AR + 32768)
    xTst = at("xTst", [128, 8 * 512], BF16, AR + 40960)

    s_sb = at("s_sb", [128, 2 * 4096], BF16, AR)
    w_sb = at("w_sb", [128, 2 * 4096], BF16, AR + 16384)
    xf_t = at("xf_t", [128, D], F32, AR + 32768)
    xb_t = at("xb_t", [128, D], BF16, AR + 36864)
    xT_q = at("xT_q", [128, 8 * 128], BF16, AR + 38912)
    Qt = at("Qt", [128, 8 * 128], BF16, AR + 40960)
    attnT = at("attnT", [128, 8 * 128], BF16, AR + 43008)
    r_sb = at("r_sb", [128, D], F32, AR)
    scr4 = at("scr4", [128, D], F32, AR + 4096)
    wo_sl = at("wo_sl", [128, D], BF16, AR + 8192)

    x1T_s = at("x1T_s", [128, 8 * NTOK], BF16, BG)
    gate_s = at("gate_s", [128, 8 * IFFN], BF16, BG + 32768)
    up_s = at("up_s", [128, 8 * IFFN], BF16, BG + 75776)
    down_s = at("down_s", [128, 2 * 1024], BF16, BG + 118784)
    fst_s = at("fst_s", [128, D], F32, BG + 122880)

    pA = nc.alloc_psum_tensor("pA", [128, 1024], F32)
    pC = nc.alloc_psum_tensor("pC", [128, 512], F32)
    pD = nc.alloc_psum_tensor("pD", [128, 512], F32)
    pE = nc.alloc_psum_tensor("pE", [128, 512], F32)
    pF = nc.alloc_psum_tensor("pF", [128, 512], F32)
    pG = nc.alloc_psum_tensor("pG", [128, 512], F32)
    pT = nc.alloc_psum_tensor("pT", [128, 256], BF16)

    with ExitStack() as ctx:
        E_ = ctx.enter_context
        sems = {nm: E_(nc.semaphore(nm)) for nm in
                ("sq", "gq", "TE", "VE", "SE", "GE")}
        cnts = {nm: 0 for nm in sems}
        block = E_(nc.Block())

        ops = []
        fence_req = {"sq": 0, "gq": 0}
        fence_app = {"sq": 0, "gq": 0}
        pending_bar = {}

        SELF = {"VE": "VE", "SE": "SE", "GE": "GE"}

        def op(eng, fn, waits=(), inc=()):
            w = [(s, v) for (s, v) in waits if v > 0]
            bar = pending_bar.pop(eng, None)
            if bar:
                have = dict()
                for s, v in w:
                    have[s] = max(have.get(s, 0), v)
                for s, v in bar:
                    if v > have.get(s, 0):
                        w.append((s, v))
            inc = list(inc)
            if eng in SELF:
                sn = SELF[eng]
                if cnts[sn] > 0:
                    w = [(s, v) for (s, v) in w if s != sn] + [(sn, cnts[sn])]
                if not any(s == sn for s, n in inc):
                    inc.append((sn, 1))
            ops.append([eng, fn, w, inc])
            for s, n in inc:
                cnts[s] += n

        def tk(s):
            return cnts[s]

        def dbound(q):
            fence_req[q] = cnts[q]
            return cnts[q]

        def dma(q, dst, src, waits=()):
            w = list(waits)
            if fence_req[q] > fence_app[q]:
                w.append((q, fence_req[q]))
                fence_app[q] = fence_req[q]
            op(q, lambda e, d=dst, s=src: e.dma_start(out=d, in_=s), w,
               [(q, 16)])

        def ensure_inc(engname):
            for rec in reversed(ops):
                if rec[0] == engname:
                    semn = "GE" if engname in ("gq", "GE") else engname
                    if engname in ("sq",):
                        semn = "sq"
                    if not rec[3]:
                        n = 16 if engname in ("sq", "gq") else 1
                        sn = engname if engname in ("sq", "gq") else semn
                        rec[3].append((sn, n))
                        cnts[sn] += n
                    return

        def barrier():
            for en in ("sq", "gq", "TE", "VE", "SE", "GE"):
                if any(r[0] == en for r in ops):
                    ensure_inc(en)
            dbound("sq"); dbound("gq")
            snap = {s: tk(s) for s in cnts}
            blist = [(s, v) for s, v in snap.items() if v > 0]
            for en in ("sq", "gq", "TE", "VE", "SE", "GE"):
                pending_bar[en] = [(s, v) for (s, v) in blist if s != en]
            return blist

        dma("sq", iota_s[:], d_iota)
        dma("sq", ident[:], d_ident)
        dma("gq", bq_s[:], d_bq)
        dma("gq", bk_s[:], d_bk)
        dma("gq", ab1_s[0:1, 0:AHID], d_ab1)
        dma("gq", ab2_s[0:1, 0:MAXK], d_ab2)
        dma("gq", bo_s[:], d_bo)
        dma("gq", bv_s[:], d_bv)
        dma("gq", a2_s[:], d_a2T)
        op("GE", lambda e: e.memset(ones_b[:], 1.0), [], [("GE", 1)])
        Vv4 = Vv.ap().rearrange("p (a b c) -> p a b c", b=16, c=64)
        for dc in range(8):
            dma("gq", wq_s[:, dc * 1024:(dc + 1) * 1024],
                d_wqT[dc * 128:(dc + 1) * 128, :])
            dma("gq", wk_s[:, dc * 1024:(dc + 1) * 1024],
                d_wkT[dc * 128:(dc + 1) * 128, :])
            dma("gq", wv_s[:, dc * 1024:(dc + 1) * 1024],
                d_wvT[dc * 128:(dc + 1) * 128, :])
            dma("gq", rw_s[:, dc * 8:(dc + 1) * 8],
                d_rwT[dc * 128:(dc + 1) * 128, :])
            dma("gq", a1_s[:, dc * 64: dc * 64 + AHID],
                d_a1T[dc * 128:(dc + 1) * 128, :])
        init_done = [("sq", dbound("sq")), ("gq", dbound("gq")), ("GE", tk("GE"))]

        prev_xuse = 0
        prev_vst = {}
        for st in range(NST):
            for ti in range(4):
                dma("gq", xbst[:, ti * 1024:(ti + 1) * 1024],
                    d_xs[st * 512 + ti * 128: st * 512 + (ti + 1) * 128, :],
                    waits=[("TE", prev_xuse)] if ti == 0 else [])
            ldx = dbound("gq")
            for ti in range(4):
                for j in range(8):
                    op("TE", lambda e, ti=ti, j=j: e.transpose(
                        pT[:, 0:128],
                        xbst[:, ti * 1024 + j * 128: ti * 1024 + (j + 1) * 128],
                        ident[:]),
                       [("gq", ldx), ("SE", tk("SE"))] + (init_done if (st == 0 and ti == 0 and j == 0) else []),
                       [("TE", 1)])
                    op("SE", lambda e, ti=ti, j=j: e.copy(
                        xTst[:, j * 512 + ti * 128: j * 512 + (ti + 1) * 128],
                        pT[:, 0:128]),
                       [("TE", tk("TE"))], [("SE", 1)])
            xT_done = tk("SE")
            evK = tk("SE")
            for j in range(8):
                for dc in range(8):
                    op("TE", lambda e, j=j, dc=dc: e.matmul(
                        pC[:], wk_s[:, dc * 1024 + j * 128: dc * 1024 + (j + 1) * 128],
                        xTst[:, dc * 512:(dc + 1) * 512],
                        start=(dc == 0), stop=False),
                       ([("SE", max(xT_done, evK))] +
                        (init_done if (st == 0 and j == 0) else []))
                       if dc == 0 else [], [])
                op("TE", lambda e, j=j: e.matmul(
                    pC[:], bk_s[0:1, j * 128:(j + 1) * 128],
                    ones_b[0:1, 0:512], start=False, stop=True),
                   [], [("TE", 1)])
                op("SE", lambda e, j=j, st=st: e.copy(
                    KT[:, j * 4096 + st * 512: j * 4096 + (st + 1) * 512],
                    pC[:]),
                   [("TE", tk("TE"))], [("SE", 1)])
                evK = tk("SE")
            for ti in range(4):
                tc = st * 4 + ti
                wv_prev = prev_vst.get("v", 0)
                for dc in range(8):
                    for hh in range(2):
                        op("TE", lambda e, ti=ti, dc=dc, hh=hh: e.matmul(
                            pA[:, hh * 512:(hh + 1) * 512],
                            xTst[:, dc * 512 + ti * 128: dc * 512 + (ti + 1) * 128],
                            wv_s[:, dc * 1024 + hh * 512: dc * 1024 + (hh + 1) * 512],
                            start=(dc == 0), stop=False),
                           [("SE", wv_prev)] if (dc == 0 and hh == 0) else [], [])
                for hh in range(2):
                    op("TE", lambda e, hh=hh: e.matmul(
                        pA[:, hh * 512:(hh + 1) * 512],
                        ones_b[0:1, 0:128], bv_s[0:1, hh * 512:(hh + 1) * 512],
                        start=False, stop=True),
                       [], [("TE", 1)] if hh == 1 else [])
                op("SE", lambda e, tc=tc: e.copy(
                    Vv4[:, tc, :, :],
                    pA.ap().rearrange("p (a b) -> p a b", a=16)),
                   [("TE", tk("TE"))], [("SE", 1)])
                prev_vst["v"] = tk("SE")
            prev_xuse = tk("TE")

        s1_done = barrier()
        op("VE", lambda e: e.memset(sel[:], 0.0), s1_done, [("VE", 1)])

        (MU, SIG, TAUA, ZA, TAUB, ZB, CC, BD, BT, BC, T1, T2, T3,
         AL, RK, TVAL, UB, SLO, S2HI, GE64, TPK, T4, NT, CCB) = range(24)

        def sc(q, i=None):
            if i is None:
                return sel[:, 2 * q: 2 * q + 2]
            return sel[:, 2 * q + i: 2 * q + i + 1]

        V = "VE"; SEN = "SE"; T = "TE"

        def vop(fn, waits=(), inc=True):
            op(V, fn, waits, [("VE", 1)] if inc else [])

        def sop(fn, waits=(), inc=True):
            op(SEN, fn, waits, [("SE", 1)] if inc else [])

        for qt in range(NQT):
            if qt > 0:
                barrier()
            dma("sq", xf_t[:], d_xo[qt * 128:(qt + 1) * 128, :],
                waits=[("SE", tk("SE")), ("VE", tk("VE"))])
            dbound("sq")
            dma("gq", xb_t[:], d_xo[qt * 128:(qt + 1) * 128, :],
                waits=[("TE", tk("TE"))])
            ldxb = dbound("gq")
            for j in range(8):
                op(T, lambda e, j=j: e.transpose(
                    pT[:, 0:128], xb_t[:, j * 128:(j + 1) * 128], ident[:]),
                   [("gq", ldxb), ("SE", tk("SE"))], [("TE", 1)])
                sop(lambda e, j=j: e.copy(xT_q[:, j * 128:(j + 1) * 128],
                                          pT[:, 0:128]),
                    [("TE", tk("TE"))])
            xtq = tk("SE")
            evq = 0
            for j in range(8):
                for dc in range(8):
                    op(T, lambda e, j=j, dc=dc: e.matmul(
                        pC[:, 0:128],
                        wq_s[:, dc * 1024 + j * 128: dc * 1024 + (j + 1) * 128],
                        xT_q[:, dc * 128:(dc + 1) * 128],
                        start=(dc == 0), stop=False),
                       [("SE", max(xtq, evq))] if dc == 0 else [], [])
                op(T, lambda e, j=j: e.matmul(
                    pC[:, 0:128], bq_s[0:1, j * 128:(j + 1) * 128],
                    ones_b[0:1, 0:128], start=False, stop=True),
                   [], [("TE", 1)])
                sop(lambda e, j=j: e.copy(
                    Qt[:, j * 128:(j + 1) * 128], pC[:, 0:128]),
                    [("TE", tk("TE"))])
                evq = tk("SE")

            for pj in range(8):
                hA, hB = 2 * pj, 2 * pj + 1
                pair_ve0 = tk("VE")
                pair_te0 = tk("TE")
                first_evac = [True]
                banksAB = [(pC, pD), (pE, pF)]
                ev_tok = [0, 0]
                for kc in range(8):
                    bA, bB = banksAB[kc % 2]
                    w0 = [("SE", ev_tok[kc % 2])] if kc >= 2 else []
                    op(T, lambda e, pj=pj, kc=kc, bA=bA: e.matmul(
                        bA[:], Qt[0:64, pj * 128:(pj + 1) * 128],
                        KT[0:64, pj * 4096 + kc * 512: pj * 4096 + (kc + 1) * 512],
                        start=True, stop=True, tile_position=(0, 0)),
                       w0 + ([("SE", tk("SE"))] if kc < 2 else []), [])
                    op(T, lambda e, pj=pj, kc=kc, bB=bB: e.matmul(
                        bB[:], Qt[64:128, pj * 128:(pj + 1) * 128],
                        KT[64:128, pj * 4096 + kc * 512: pj * 4096 + (kc + 1) * 512],
                        start=True, stop=True, tile_position=(64, 0)),
                       [], [("TE", 1)])
                    tec = tk("TE")
                    for i, bX in ((0, bA), (1, bB)):
                        evw = [("TE", tec)]
                        if first_evac[0]:
                            evw.append(("VE", pair_ve0))
                            first_evac[0] = False
                        sop(lambda e, i=i, kc=kc, bX=bX: e.activation(
                            out=s_sb[:, i * 4096 + kc * 512: i * 4096 + (kc + 1) * 512],
                            in_=bX[:], func=AF.Copy, scale=1.0,
                            accum_out=acc_sb[:, i * 8 + kc: i * 8 + kc + 1]),
                            evw)
                        if kc == 2:
                            sop(lambda e, i=i, bX=bX: e.activation(
                                out=scr4[:, i * 512:(i + 1) * 512], in_=bX[:],
                                func=AF.Square, scale=1.0,
                                accum_out=sq_sb[:, i:i + 1]), [])
                    ev_tok[kc % 2] = tk("SE")
                ph1 = tk("SE")

                vop(lambda e: e.tensor_reduce(
                    sc(MU), acc_sb.ap().rearrange("p (a b) -> p a b", a=2),
                    axis=AX.X, op=ALU.add), [("SE", ph1), ("TE", pair_te0)])
                vop(lambda e: e.tensor_scalar(sc(MU), sc(MU), 1.0 / 4096, None,
                                              op0=ALU.mult))
                vop(lambda e: e.tensor_mul(sc(T1), sc(MU), sc(MU)))
                vop(lambda e: e.scalar_tensor_tensor(
                    sc(T1), sq_sb[:], 1.0 / 512, sc(T1),
                    op0=ALU.mult, op1=ALU.subtract))
                vop(lambda e: e.tensor_scalar(sc(T1), sc(T1), 1e-4, None,
                                              op0=ALU.max))
                sop(lambda e: e.activation(out=sc(T2), in_=sc(T1), func=AF.Ln,
                                           scale=1.0), [("VE", tk("VE"))])
                sop(lambda e: e.activation(out=sc(SIG), in_=sc(T2), func=AF.Exp,
                                           scale=0.5))
                sseed = tk("SE")
                vop(lambda e: e.scalar_tensor_tensor(
                    sc(TAUA), sc(SIG), 2.1535, sc(MU),
                    op0=ALU.mult, op1=ALU.add), [("SE", sseed)])
                vop(lambda e: e.tensor_scalar(sc(SLO), sc(SIG), 0.5, None,
                                              op0=ALU.mult))
                vop(lambda e: e.tensor_scalar(sc(S2HI), sc(SIG), 2.0, None,
                                              op0=ALU.mult))
                vop(lambda e: e.memset(sc(BD), BIG))

                def best_track(CCq, TAUq):
                    vop(lambda e, CCq=CCq: e.tensor_scalar(
                        sc(T1), sc(CCq), 64.0, None, op0=ALU.subtract))
                    vop(lambda e: e.tensor_mul(sc(T2), sc(T1), sc(T1)))
                    vop(lambda e: e.tensor_tensor(sc(T3), sc(T2), sc(BD),
                                                  op=ALU.is_lt))
                    vop(lambda e: e.tensor_tensor(sc(BD), sc(T2), sc(BD),
                                                  op=ALU.min))
                    vop(lambda e, TAUq=TAUq: e.tensor_tensor(
                        sc(T4), sc(TAUq), sc(BT), op=ALU.subtract))
                    vop(lambda e: e.tensor_mul(sc(T4), sc(T4), sc(T3)))
                    vop(lambda e: e.tensor_add(sc(BT), sc(BT), sc(T4)))
                    vop(lambda e, CCq=CCq: e.tensor_tensor(
                        sc(T4), sc(CCq), sc(BC), op=ALU.subtract))
                    vop(lambda e: e.tensor_mul(sc(T4), sc(T4), sc(T3)))
                    vop(lambda e: e.tensor_add(sc(BC), sc(BC), sc(T4)))

                bt_pend = None
                for r in range(RROUNDS):
                    TAU, Z = (TAUA, ZA) if r % 2 == 0 else (TAUB, ZB)
                    TAUP, ZP = (TAUB, ZB) if r % 2 == 0 else (TAUA, ZA)
                    CCr = CC if r % 2 == 0 else CCB
                    vop(lambda e, TAU=TAU: e.tensor_scalar(
                        sc(NT), sc(TAU), -1.0, None, op0=ALU.mult))
                    veNT = tk("VE")
                    sop(lambda e: e.activation(
                        out=w_sb[:, 4096:8192], in_=s_sb[:, 4096:8192],
                        func=AF.Sign, bias=sc(NT, 1), scale=1.0,
                        accum_out=sq_sb[:, 0:1]), [("VE", veNT)])
                    seCnt = tk("SE")
                    vop(lambda e, TAU=TAU, CCr=CCr: e.tensor_scalar(
                        w_sb[:, 0:4096], s_sb[:, 0:4096],
                        sc(TAU, 0), None, op0=ALU.is_ge, op1=ALU.add,
                        accum_out=sc(CCr, 0)))
                    if bt_pend is not None:
                        best_track(*bt_pend)
                    vop(lambda e, CCr=CCr: e.tensor_scalar(
                        sc(CCr, 1), sq_sb[:, 0:1], 0.5, 2048.0,
                        op0=ALU.mult, op1=ALU.add), [("SE", seCnt)])
                    bt_pend = (CCr, TAU)
                    if r == RROUNDS - 1:
                        best_track(*bt_pend)
                        break
                    vop(lambda e, CCr=CCr: e.tensor_scalar(
                        sc(T1), sc(CCr), 0.7, 1300.0,
                        op0=ALU.max, op1=ALU.min))
                    sop(lambda e: e.activation(out=sc(T2), in_=sc(T1),
                                               func=AF.Ln, scale=1.0),
                        [("VE", tk("VE"))])
                    lnr = tk("SE")
                    vop(lambda e: e.tensor_scalar(sc(T1), sc(T2), ZC3, ZC2,
                                                  op0=ALU.mult, op1=ALU.add),
                        [("SE", lnr)])
                    vop(lambda e: e.tensor_mul(sc(T1), sc(T1), sc(T2)))
                    vop(lambda e, Z=Z: e.scalar_tensor_tensor(
                        sc(Z), sc(T1), ZC1, sc(T2), op0=ALU.add, op1=ALU.mult))
                    if r == 0:
                        vop(lambda e, Z=Z: e.tensor_scalar(
                            sc(T1), sc(Z), ZC0, None, op0=ALU.add))
                        vop(lambda e: e.reciprocal(sc(T2), sc(T1)))
                        vop(lambda e: e.tensor_mul(sc(T2), sc(T2), sc(SIG)))
                        vop(lambda e: e.tensor_scalar(sc(T2), sc(T2), 2.1535,
                                                      None, op0=ALU.mult))
                    else:
                        vop(lambda e, Z=Z, ZP=ZP: e.tensor_tensor(
                            sc(T1), sc(Z), sc(ZP), op=ALU.subtract))
                        vop(lambda e: e.tensor_mul(sc(T2), sc(T1), sc(T1)))
                        vop(lambda e: e.tensor_scalar(sc(T2), sc(T2), 1e-8,
                                                      None, op0=ALU.add))
                        vop(lambda e: e.reciprocal(sc(T4), sc(T2)))
                        vop(lambda e: e.tensor_mul(sc(T1), sc(T1), sc(T4)))
                        vop(lambda e, TAU=TAU, TAUP=TAUP: e.tensor_tensor(
                            sc(T2), sc(TAU), sc(TAUP), op=ALU.subtract))
                        vop(lambda e: e.tensor_mul(sc(T2), sc(T2), sc(T1)))
                    vop(lambda e: e.tensor_tensor(sc(T2), sc(T2), sc(S2HI),
                                                  op=ALU.min))
                    vop(lambda e: e.tensor_tensor(sc(T2), sc(T2), sc(SLO),
                                                  op=ALU.max))
                    vop(lambda e, Z=Z: e.tensor_scalar(
                        sc(T1), sc(Z), Z64T, None, op0=ALU.subtract))
                    vop(lambda e: e.tensor_mul(sc(T1), sc(T1), sc(T2)))
                    vop(lambda e, TAU=TAU, TAUP=TAUP: e.tensor_tensor(
                        sc(TAUP), sc(TAU), sc(T1), op=ALU.subtract))

                vop(lambda e: e.tensor_scalar(sc(GE64), sc(BC), 64.0, None,
                                              op0=ALU.is_ge))
                vop(lambda e: e.tensor_scalar(sc(AL), sc(GE64), -2.0, 1.0,
                                              op0=ALU.mult, op1=ALU.add))
                vop(lambda e: e.tensor_scalar(sc(T1), sc(BC), 64.0, None,
                                              op0=ALU.subtract))
                vop(lambda e: e.tensor_scalar(sc(T2), sc(BC), -1.0, 63.0,
                                              op0=ALU.mult, op1=ALU.add))
                vop(lambda e: e.tensor_tensor(sc(T4), sc(T1), sc(T2),
                                              op=ALU.subtract))
                vop(lambda e: e.tensor_mul(sc(T4), sc(T4), sc(GE64)))
                vop(lambda e: e.tensor_add(sc(RK), sc(T2), sc(T4)))
                vop(lambda e: e.tensor_scalar(sc(RK), sc(RK), 0.0, 7.0,
                                              op0=ALU.max, op1=ALU.min))
                vop(lambda e: e.tensor_mul(sc(UB), sc(AL), sc(BT)))
                for i in range(2):
                    vop(lambda e, i=i: e.tensor_scalar(
                        w_sb[:, 4096:8192], s_sb[:, i * 4096:(i + 1) * 4096],
                        sc(AL, i), None, op0=ALU.mult))
                    vop(lambda e, i=i: e.tensor_scalar(
                        w_sb[:, 0:4096], w_sb[:, 4096:8192], sc(UB, i), BIG,
                        op0=ALU.is_ge, op1=ALU.mult))
                    vop(lambda e: e.tensor_sub(w_sb[:, 4096:8192],
                                               w_sb[:, 4096:8192],
                                               w_sb[:, 0:4096]))
                    vop(lambda e, i=i: e.max(out=d8_sb[:, i * 8:(i + 1) * 8],
                                             in_=w_sb[:, 4096:8192]))
                    vop(lambda e, i=i: e.tensor_copy(
                        rt_t[:, 8:16], d8_sb[:, i * 8:(i + 1) * 8]))
                    vop(lambda e, i=i: e.tensor_scalar(
                        rt_t[:, 0:8], iota_s[:, 0:8], sc(RK, i), None,
                        op0=ALU.is_equal))
                    vop(lambda e: e.tensor_mul(rt_t[:, 16:24], rt_t[:, 0:8],
                                               rt_t[:, 8:16]))
                    vop(lambda e, i=i: e.reduce_sum(sc(TPK, i), rt_t[:, 16:24],
                                                    axis=AX.X))
                vop(lambda e: e.tensor_mul(sc(TVAL), sc(TPK), sc(AL)))

                for i, h in ((0, hA), (1, hB)):
                    wreg = w_sb[:, i * 4096:(i + 1) * 4096]
                    wtreg = s_sb[:, i * 4096:(i + 1) * 4096]
                    vop(lambda e, i=i: e.max(out=d8_sb[:, i * 8:(i + 1) * 8],
                                             in_=s_sb[:, i * 4096:(i + 1) * 4096]))
                    vop(lambda e, i=i: e.tensor_copy(
                        sc(T1, i), d8_sb[:, i * 8: i * 8 + 1]))
                    vop(lambda e, i=i: e.tensor_scalar(
                        sc(T2, i), sc(T1, i), -650.0, None, op0=ALU.add))
                    vop(lambda e, i=i: e.tensor_tensor(
                        sc(T2, i), sc(T2, i), sc(TVAL, i), op=ALU.max))
                    vop(lambda e, i=i: e.tensor_scalar(
                        sc(T3, i), sc(T2, i), -0.125, None, op0=ALU.mult))
                    veSH = tk("VE")
                    sop(lambda e, i=i, wreg=wreg: e.activation(
                        out=wreg, in_=s_sb[:, i * 4096:(i + 1) * 4096],
                        func=AF.Exp, scale=0.125, bias=sc(T3, i)),
                        [("VE", veSH)])
                    seE = tk("SE")
                    vop(lambda e, i=i, wreg=wreg: e.scalar_tensor_tensor(
                        wreg, s_sb[:, i * 4096:(i + 1) * 4096], sc(TVAL, i),
                        wreg, op0=ALU.is_ge, op1=ALU.mult,
                        accum_out=sc(T4, i)), [("SE", seE)])
                    vop(lambda e, i=i: e.reciprocal(sc(T1, i), sc(T4, i)))
                    vop(lambda e, i=i, wreg=wreg: e.tensor_scalar(
                        wreg, wreg, sc(T1, i), None, op0=ALU.mult))
                    wnorm = tk("VE")
                    ev_tok2 = [0, 0]
                    wv_done = 0
                    for kc2 in range(32):
                        sl = kc2 % 2
                        op(T, lambda e, kc2=kc2, sl=sl, wreg=wreg: e.transpose(
                            pT[:, sl * 128:(sl + 1) * 128],
                            wreg[:, kc2 * 128:(kc2 + 1) * 128], ident[:]),
                           [("VE", wnorm), ("SE", ev_tok2[sl])], [("TE", 1)])
                        trk = tk("TE")
                        sop(lambda e, kc2=kc2, sl=sl, wtreg=wtreg: e.copy(
                            wtreg[:, kc2 * 128:(kc2 + 1) * 128],
                            pT[:, sl * 128:(sl + 1) * 128]),
                            [("TE", trk)])
                        ev_tok2[sl] = tk("SE")
                        op(T, lambda e, kc2=kc2, h=h, wtreg=wtreg: e.matmul(
                            pG[0:64, 0:128], Vv4[:, kc2, h, 0:64],
                            wtreg[:, kc2 * 128:(kc2 + 1) * 128],
                            start=(kc2 == 0), stop=(kc2 == 31)),
                           [("SE", ev_tok2[sl])],
                           [("TE", 1)] if kc2 == 31 else [])
                    teW = tk("TE")
                    sop(lambda e, h=h: e.copy(
                        attnT[64 * (h % 2):64 * (h % 2) + 64,
                              (h // 2) * 128:(h // 2) * 128 + 128],
                        pG[0:64, 0:128]), [("TE", teW)])

            attn_done = tk("SE")
            opj_prev = 0
            for j in range(8):
                dma("gq", wo_sl[:], d_woT[j * 128:(j + 1) * 128, :],
                    waits=[("TE", opj_prev)] +
                          ([("SE", attn_done), ("VE", tk("VE"))] if j == 0 else []))
                ldw = dbound("gq")
                for hh in range(2):
                    op(T, lambda e, j=j, hh=hh: e.matmul(
                        pA[:, hh * 512:(hh + 1) * 512],
                        attnT[:, j * 128:(j + 1) * 128],
                        wo_sl[:, hh * 512:(hh + 1) * 512],
                        start=(j == 0), stop=False),
                       ([("gq", ldw), ("SE", attn_done)] +
                        ([("VE", tk("VE"))] if j == 0 else []))
                       if hh == 0 else [],
                       [("TE", 1)] if hh == 1 else [])
                opj_prev = tk("TE")
            for hh in range(2):
                op(T, lambda e, hh=hh: e.matmul(
                    pA[:, hh * 512:(hh + 1) * 512],
                    ones_b[0:1, 0:128], bo_s[0:1, hh * 512:(hh + 1) * 512],
                    start=False, stop=True), [], [("TE", 1)] if hh == 1 else [])
            teO = tk("TE")
            vop(lambda e: e.scalar_tensor_tensor(
                r_sb[:], xf_t[:], 2.0, pA.ap(),
                op0=ALU.mult, op1=ALU.add, accum_out=lnst[:, 0:1]),
                [("TE", teO), ("sq", dbound("sq"))])
            veR = tk("VE")
            sop(lambda e: e.activation(out=scr4[:], in_=r_sb[:], func=AF.Square,
                                       scale=1.0, accum_out=lnst[:, 1:2]),
                [("VE", veR)])
            seR = tk("SE")
            vop(lambda e: e.tensor_scalar(lnst[:, 2:3], lnst[:, 0:1],
                                          1.0 / 1024, None, op0=ALU.mult),
                [("SE", seR)])
            vop(lambda e: e.tensor_mul(lnst[:, 3:4], lnst[:, 2:3], lnst[:, 2:3]))
            vop(lambda e: e.scalar_tensor_tensor(
                lnst[:, 3:4], lnst[:, 1:2], 1.0 / 1024, lnst[:, 3:4],
                op0=ALU.mult, op1=ALU.subtract))
            vop(lambda e: e.tensor_scalar(lnst[:, 3:4], lnst[:, 3:4], LN_EPS,
                                          None, op0=ALU.add))
            sop(lambda e: e.activation(out=lnst[:, 4:5], in_=lnst[:, 3:4],
                                       func=AF.Ln, scale=1.0),
                [("VE", tk("VE"))])
            sop(lambda e: e.activation(out=lnst[:, 5:6], in_=lnst[:, 4:5],
                                       func=AF.Exp, scale=-0.5))
            seV = tk("SE")
            vop(lambda e: e.tensor_scalar(scr4[:], r_sb[:], lnst[:, 2:3],
                                          lnst[:, 5:6], op0=ALU.subtract,
                                          op1=ALU.mult), [("SE", seV)])
            vop(lambda e: e.tensor_scalar(xb_t[:], r_sb[:], lnst[:, 2:3],
                                          lnst[:, 5:6], op0=ALU.subtract,
                                          op1=ALU.mult))
            seX1 = tk("VE")
            dma("sq", d_x1f[qt * 128:(qt + 1) * 128, :], scr4[:],
                waits=[("VE", seX1)])
            for j in range(8):
                op(T, lambda e, j=j: e.transpose(
                    pT[:, 0:128], xb_t[:, j * 128:(j + 1) * 128], ident[:]),
                   ([("VE", seX1)] if j == 0 else []) + [("SE", tk("SE"))],
                   [("TE", 1)])
                sop(lambda e, j=j: e.copy(xT_q[:, j * 128:(j + 1) * 128],
                                          pT[:, 0:128]),
                    [("TE", tk("TE"))])
            seXT = tk("SE")
            for j in range(8):
                dma("sq", d_x1bT[j * 128:(j + 1) * 128,
                                 qt * 128:(qt + 1) * 128],
                    xT_q[:, j * 128:(j + 1) * 128],
                    waits=[("SE", seXT)] if j == 0 else [])

            for dc in range(8):
                op(T, lambda e, dc=dc: e.matmul(
                    pC[:, 0:8], xT_q[:, dc * 128:(dc + 1) * 128],
                    rw_s[:, dc * 8:(dc + 1) * 8],
                    start=(dc == 0), stop=(dc == 7)),
                   [("SE", seXT), ("VE", tk("VE"))] if dc == 0 else [],
                   [("TE", 1)] if dc == 7 else [])
            teR = tk("TE")
            vop(lambda e: e.reduce_max(rt_t[:, 8:9], pC[:, 0:8], axis=AX.X),
                [("TE", teR)])
            vop(lambda e: e.tensor_scalar(rt_t[:, 8:9], rt_t[:, 8:9], -1.0,
                                          None, op0=ALU.mult))
            veM = tk("VE")
            sop(lambda e: e.activation(out=rt_t[:, 0:8], in_=pC[:, 0:8],
                                       func=AF.Exp, bias=rt_t[:, 8:9],
                                       scale=1.0, accum_out=rt_t[:, 9:10]),
                [("VE", veM)])
            seP = tk("SE")
            vop(lambda e: e.reciprocal(rt_t[:, 10:11], rt_t[:, 9:10]),
                [("SE", seP)])
            vop(lambda e: e.tensor_scalar(rt_t[:, 0:8], rt_t[:, 0:8],
                                          rt_t[:, 10:11], None, op0=ALU.mult))
            probs_done = tk("VE")

            for dc in range(8):
                op(T, lambda e, dc=dc: e.matmul(
                    pD[0:AHID, 0:128], a1_s[:, dc * 64: dc * 64 + AHID],
                    xT_q[:, dc * 128:(dc + 1) * 128],
                    start=(dc == 0), stop=False),
                   [("TE", teR)] if dc == 0 else [], [])
            op(T, lambda e: e.matmul(
                pD[0:AHID, 0:128], ab1_s[0:1, 0:AHID], ones_b[0:1, 0:128],
                start=False, stop=True), [], [("TE", 1)])
            teA = tk("TE")
            sop(lambda e: e.copy(ac_f[:], pD[0:AHID, 0:128]), [("TE", teA)])
            seA1 = tk("SE")
            sop(lambda e: e.activation(out=ac_g[:], in_=ac_f[:],
                                       func=AF.Square, scale=1.0))
            seA2 = tk("SE")
            vop(lambda e: e.tensor_mul(ac_g[:], ac_g[:], ac_f[:]),
                [("SE", seA2)])
            vop(lambda e: e.scalar_tensor_tensor(
                ac_g[:], ac_g[:], 0.044715, ac_f[:],
                op0=ALU.mult, op1=ALU.add))
            veA = tk("VE")
            sop(lambda e: e.activation(out=ac_g[:], in_=ac_g[:], func=AF.Tanh,
                                       scale=GELU_C), [("VE", veA)])
            seA3 = tk("SE")
            vop(lambda e: e.scalar_tensor_tensor(
                ac_g[:], ac_g[:], 1.0, ac_f[:], op0=ALU.add, op1=ALU.mult),
                [("SE", seA3)])
            vop(lambda e: e.tensor_scalar(ac_b[:], ac_g[:], 0.5, None,
                                          op0=ALU.mult))
            veG = tk("VE")
            op(T, lambda e: e.matmul(pC[0:MAXK, 128:256], a2_s[:], ac_b[:],
                                     start=True, stop=False),
               [("VE", veG)], [])
            op(T, lambda e: e.matmul(
                pC[0:MAXK, 128:256], ab2_s[0:1, 0:MAXK], ones_b[0:1, 0:128],
                start=False, stop=True), [], [("TE", 1)])
            teL = tk("TE")
            sop(lambda e: e.copy(ac_f[0:MAXK, 0:128], pC[0:MAXK, 128:256]),
                [("TE", teL)])
            seL = tk("SE")
            vop(lambda e: e.tensor_scalar(ac_b[0:MAXK, 0:128],
                                          ac_f[0:MAXK, 0:128], 30.0, -30.0,
                                          op0=ALU.min, op1=ALU.max),
                [("SE", seL)])
            veL = tk("VE")
            op(T, lambda e: e.transpose(pT[0:128, 0:MAXK],
                                        ac_b[0:MAXK, 0:128], ident[0:MAXK, 0:MAXK]),
               [("VE", veL), ("SE", tk("SE"))], [("TE", 1)])
            sop(lambda e: e.copy(lt_f[:], pT[0:128, 0:MAXK]),
                [("TE", tk("TE"))])
            seLT = tk("SE")
            vop(lambda e: e.reduce_max(rt_t[:, 11:12], lt_f[:], axis=AX.X),
                [("SE", seLT)])
            vop(lambda e: e.tensor_scalar(rt_t[:, 16:22], lt_f[:],
                                          rt_t[:, 11:12], BIG,
                                          op0=ALU.is_lt, op1=ALU.mult))
            vop(lambda e: e.tensor_add(rt_t[:, 16:22], rt_t[:, 16:22],
                                       iota_s[:, 0:MAXK]))
            vop(lambda e: e.tensor_reduce(rt_t[:, 12:13], rt_t[:, 16:22],
                                          axis=AX.X, op=ALU.min))
            vop(lambda e: e.memset(rt_t[:, 16:24], 0.0))
            for ee in range(8):
                vop(lambda e, ee=ee: e.tensor_scalar(
                    acc_sb[:, 0:8], rt_t[:, 0:8], rt_t[:, ee:ee + 1], None,
                    op0=ALU.is_gt))
                vop(lambda e: e.tensor_add(rt_t[:, 16:24], rt_t[:, 16:24],
                                           acc_sb[:, 0:8]))
            vop(lambda e: e.tensor_scalar(rt_t[:, 16:24], rt_t[:, 16:24],
                                          -1.0, 7.0, op0=ALU.mult,
                                          op1=ALU.add))
            vop(lambda e: e.tensor_scalar(rt_t[:, 16:24], rt_t[:, 16:24],
                                          rt_t[:, 12:13], None, op0=ALU.is_le))
            vop(lambda e, qt=qt: e.tensor_tensor(
                gsc[:, qt * 8:(qt + 1) * 8], rt_t[:, 16:24], rt_t[:, 0:8],
                op=ALU.mult))

        s2a_done = barrier()

        for dc in range(8):
            dma("sq", x1T_s[:, dc * NTOK:(dc + 1) * NTOK],
                d_x1bT[dc * 128:(dc + 1) * 128, :], waits=s2a_done if dc == 0 else [])
        x1T_ld = dbound("sq")
        facc_tok = [0] * NQT
        gu_prev = 0
        fst_prev = 0
        exp_te = 0
        for e_i in range(NEXP):
            for dc in range(8):
                dma("gq", gate_s[:, dc * IFFN:(dc + 1) * IFFN],
                    d_gT[e_i, dc * 128:(dc + 1) * 128, :],
                    waits=([("TE", exp_te)] if dc == 0 else []))
                dma("gq", up_s[:, dc * IFFN:(dc + 1) * IFFN],
                    d_uT[e_i, dc * 128:(dc + 1) * 128, :])
            w_ld = dbound("gq")
            dn_prev = [0, 0]
            dn_evac = 0
            for tb in range(8):
                for ic in range(NIC):
                    dbuf = ic % 2
                    dma("gq", down_s[:, dbuf * 1024:(dbuf + 1) * 1024],
                        d_dT[e_i, ic * 128:(ic + 1) * 128, :],
                        waits=[("TE", dn_prev[dbuf])])
                    d_ld = dbound("gq")
                    for dc in range(8):
                        op(T, lambda e, e_i=e_i, tb=tb, ic=ic, dc=dc: e.matmul(
                            pC[:, 0:256],
                            gate_s[:, dc * IFFN + ic * 128: dc * IFFN + (ic + 1) * 128],
                            x1T_s[:, dc * NTOK + tb * 256: dc * NTOK + (tb + 1) * 256],
                            start=(dc == 0), stop=(dc == 7)),
                           ([("gq", w_ld), ("sq", x1T_ld), ("SE", tk("SE"))]
                            if dc == 0 else []), [])
                        op(T, lambda e, e_i=e_i, tb=tb, ic=ic, dc=dc: e.matmul(
                            pD[:, 0:256],
                            up_s[:, dc * IFFN + ic * 128: dc * IFFN + (ic + 1) * 128],
                            x1T_s[:, dc * NTOK + tb * 256: dc * NTOK + (tb + 1) * 256],
                            start=(dc == 0), stop=(dc == 7)),
                           [("VE", gu_prev)] if dc == 0 else [],
                           [("TE", 1)] if dc == 7 else [])
                    teGU = tk("TE")
                    sop(lambda e: e.activation(out=sg_s[:], in_=pC[:, 0:256],
                                               func=AF.Silu, scale=1.0),
                        [("TE", teGU)])
                    seS = tk("SE")
                    vop(lambda e: e.tensor_tensor(gu_s[:], sg_s[:],
                                                  pD[:, 0:256], op=ALU.mult),
                        [("SE", seS), ("TE", dn_prev[1 - dbuf])])
                    gu_prev = tk("VE")
                    for q4 in range(2):
                        for hh in range(2):
                            op(T, lambda e, ic=ic, q4=q4, hh=hh, dbuf=dbuf: e.matmul(
                                (pA[:, hh * 512:(hh + 1) * 512]) if q4 == 0 else
                                (pG[:, 0:512] if hh == 0 else pF[:, 0:512]),
                                gu_s[:, q4 * 128:(q4 + 1) * 128],
                                down_s[:, dbuf * 1024 + hh * 512: dbuf * 1024 + (hh + 1) * 512],
                                start=(ic == 0), stop=(ic == NIC - 1)),
                               ([("VE", gu_prev), ("gq", d_ld)] +
                                ([("SE", dn_evac)] if ic == 0 else []))
                               if (q4 == 0 and hh == 0) else [],
                               [("TE", 1)] if (q4 == 1 and hh == 1) else [])
                    dn_prev[dbuf] = tk("TE")
                teD = tk("TE")
                for q4 in range(2):
                    qtt = tb * 2 + q4
                    srcs = [pA.ap()] if q4 == 0 else [pG.ap(), pF.ap()]
                    if q4 == 0:
                        vop(lambda e, e_i=e_i, qtt=qtt: e.tensor_scalar(
                            fst_s[:], pA[:],
                            gsc[:, qtt * 8 + e_i: qtt * 8 + e_i + 1], None,
                            op0=ALU.mult),
                            [("TE", teD), ("gq", max(facc_tok[qtt], fst_prev))])
                    else:
                        vop(lambda e, e_i=e_i, qtt=qtt: e.tensor_scalar(
                            fst_s[:, 0:512], pG[:, 0:512],
                            gsc[:, qtt * 8 + e_i: qtt * 8 + e_i + 1], None,
                            op0=ALU.mult),
                            [("TE", teD), ("gq", max(facc_tok[qtt], fst_prev))])
                        vop(lambda e, e_i=e_i, qtt=qtt: e.tensor_scalar(
                            fst_s[:, 512:1024], pF[:, 0:512],
                            gsc[:, qtt * 8 + e_i: qtt * 8 + e_i + 1], None,
                            op0=ALU.mult), [])
                    seF = tk("VE")
                    fw = [("VE", seF)]
                    if fence_req["gq"] > fence_app["gq"]:
                        fw.append(("gq", fence_req["gq"]))
                        fence_app["gq"] = fence_req["gq"]
                    op("gq", lambda e, e_i=e_i, qtt=qtt: e.dma_start(
                        out=d_facc[qtt * 128:(qtt + 1) * 128, :], in_=fst_s[:],
                        accum_op=(ALU.bypass if e_i == 0 else ALU.add)),
                       fw, [("gq", 16)])
                    facc_tok[qtt] = dbound("gq")
                    fst_prev = facc_tok[qtt]
            exp_te = tk("TE")

        s2b_done = barrier()

        prev_ln = 0
        for qt in range(NQT):
            dma("sq", xf_t[:], d_x1f[qt * 128:(qt + 1) * 128, :],
                waits=(s2b_done if qt == 0 else []) + [("SE", prev_ln),
                                                       ("VE", tk("VE"))])
            dma("sq", r_sb[:], d_facc[qt * 128:(qt + 1) * 128, :])
            ld2 = dbound("sq")
            vop(lambda e: e.scalar_tensor_tensor(
                scr4[:], xf_t[:], 1.0, r_sb[:], op0=ALU.mult, op1=ALU.add,
                accum_out=lnst[:, 0:1]), [("sq", ld2), ("SE", prev_ln)])
            veR = tk("VE")
            sop(lambda e: e.activation(out=r_sb[:], in_=scr4[:], func=AF.Square,
                                       scale=1.0, accum_out=lnst[:, 1:2]),
                [("VE", veR)])
            seR = tk("SE")
            vop(lambda e: e.tensor_scalar(lnst[:, 2:3], lnst[:, 0:1],
                                          1.0 / 1024, None, op0=ALU.mult),
                [("SE", seR)])
            vop(lambda e: e.tensor_mul(lnst[:, 3:4], lnst[:, 2:3], lnst[:, 2:3]))
            vop(lambda e: e.scalar_tensor_tensor(
                lnst[:, 3:4], lnst[:, 1:2], 1.0 / 1024, lnst[:, 3:4],
                op0=ALU.mult, op1=ALU.subtract))
            vop(lambda e: e.tensor_scalar(lnst[:, 3:4], lnst[:, 3:4], LN_EPS,
                                          None, op0=ALU.add))
            sop(lambda e: e.activation(out=lnst[:, 4:5], in_=lnst[:, 3:4],
                                       func=AF.Ln, scale=1.0),
                [("VE", tk("VE"))])
            sop(lambda e: e.activation(out=lnst[:, 5:6], in_=lnst[:, 4:5],
                                       func=AF.Exp, scale=-0.5))
            seV = tk("SE")
            vop(lambda e: e.tensor_scalar(xf_t[:], scr4[:], lnst[:, 2:3],
                                          lnst[:, 5:6], op0=ALU.subtract,
                                          op1=ALU.mult), [("SE", seV)])
            prev_ln2 = tk("VE")
            dma("sq", d_y[qt * 128:(qt + 1) * 128, :], xf_t[:],
                waits=[("VE", prev_ln2)])
            dbound("sq")

        op("GE", lambda e: e.memset(ones_b[0:1, 0:1], 1.0),
           [("sq", tk("sq")), ("gq", tk("gq")), ("SE", tk("SE")),
            ("VE", tk("VE")), ("TE", tk("TE"))], [("GE", 1)])

        def stream(names):
            def body(eng):
                for engname, fn, waits, incs in ops:
                    if engname not in names:
                        continue
                    for s, v in waits:
                        eng.wait_ge(sems[s], v)
                    instr = fn(eng)
                    for s, n in incs:
                        instr.then_inc(sems[s], n)
            return body

        block.sync(stream(("sq",)))
        block.tensor(stream(("TE",)))
        block.vector(stream(("VE",)))
        block.scalar(stream(("SE",)))
        block.gpsimd(stream(("gq", "GE")))

    return nc


def _stage_inputs(inputs):
    f = np.float32
    qkv_w = np.asarray(inputs["L0_attn_w_qkv_w"], f).reshape(H, 3, DH, D)
    qkv_b = np.asarray(inputs["L0_attn_w_qkv_b"], f).reshape(H, 3, DH)
    wq = np.ascontiguousarray(qkv_w[:, 0].reshape(D, D).T)
    wk = np.ascontiguousarray(qkv_w[:, 1].reshape(D, D).T)
    wv = np.ascontiguousarray(qkv_w[:, 2].reshape(D, D).T)
    bq = np.ascontiguousarray(qkv_b[:, 0].reshape(1, D))
    bk = np.ascontiguousarray(qkv_b[:, 1].reshape(1, D))
    bv = np.ascontiguousarray(qkv_b[:, 2].reshape(1, D))
    woT = np.ascontiguousarray(np.asarray(inputs["L0_attn_w_out_w"], f).T)
    bo = np.ascontiguousarray(np.asarray(inputs["L0_attn_w_out_b"], f).reshape(1, D))
    rwT = np.ascontiguousarray(np.asarray(inputs["L0_ffn_router_w"], f).T)
    a1T = np.ascontiguousarray(np.asarray(inputs["L0_ffn_actor_w1"], f).T)
    ab1 = np.ascontiguousarray(np.asarray(inputs["L0_ffn_actor_b1"], f).reshape(1, AHID))
    a2T = np.ascontiguousarray(np.asarray(inputs["L0_ffn_actor_w2"], f).T)
    ab2 = np.ascontiguousarray(np.asarray(inputs["L0_ffn_actor_b2"], f).reshape(1, MAXK))
    gT = np.ascontiguousarray(np.asarray(inputs["L0_ffn_gate_w"], f).transpose(0, 2, 1))
    uT = np.ascontiguousarray(np.asarray(inputs["L0_ffn_up_w"], f).transpose(0, 2, 1))
    dT = np.ascontiguousarray(np.asarray(inputs["L0_ffn_down_w"], f).transpose(0, 2, 1))
    iota = np.broadcast_to(np.arange(16, dtype=f), (128, 16)).copy()
    ident = np.eye(128, dtype=f).astype(nbf)
    com = dict(wqT=wq, wkT=wk, wvT=wv, bq=bq, bk=bk, bv=bv, woT=woT, bo=bo,
               rwT=rwT, a1T=a1T, ab1=ab1, a2T=a2T, ab2=ab2,
               gT=gT, uT=uT, dT=dT, iota=iota, ident=ident)
    x = np.asarray(inputs["L0_attn_x"], f)
    in_maps = []
    for c in range(8):
        b, hf = c // 2, c % 2
        m = dict(com)
        m["xseq"] = np.ascontiguousarray(x[b])
        m["xown"] = np.ascontiguousarray(x[b, hf * NTOK:(hf + 1) * NTOK])
        in_maps.append(m)
    return in_maps


def kernel(**inputs):
    in_maps = _stage_inputs(inputs)
    nc = build_nc()
    res = run_bass_kernel_spmd(nc, in_maps, core_ids=list(range(8)))
    y = np.empty((B, S, D), np.float32)
    for c in range(8):
        b, hf = c // 2, c % 2
        y[b, hf * NTOK:(hf + 1) * NTOK] = res.results[c]["y"]
    return y
